# version 01
# baseline (speedup 1.0000x reference)
import numpy as np
import ml_dtypes
import concourse.bass as bass
import concourse.mybir as mybir
from concourse.bass_utils import run_bass_kernel_spmd

F32 = mybir.dt.float32
BF16 = mybir.dt.bfloat16
I8 = mybir.dt.int8
ALU = mybir.AluOpType
ACT = mybir.ActivationFunctionType
AXX = mybir.AxisListType.X

D, FF, NH, HD, S, SOWN = 2048, 8192, 16, 128, 2048, 1024
NT, NTF = 8, 16
ISQ = 0.08838834764831845
BF = ml_dtypes.bfloat16


class Ch:
    def __init__(self, nc, name):
        self.cm = nc.semaphore(name)
        self.sem = self.cm.__enter__()
        self.n = 0


class DmaGrp:

    def __init__(self, nc, name, nsub=4):
        self.subs = [Ch(nc, f"{name}{i}") for i in range(nsub)]
        self.hist = []
        self.n = 0

    def reset(self):
        self.n = 0
        self.hist = []
        for s in self.subs:
            s.n = 0

    def watermark(self, val):
        k = min(val // 16, len(self.hist))
        if k <= 0:
            return [0] * len(self.subs)
        return self.hist[k - 1]


class Emit:
    def __init__(self, nc, live, eng, chans):
        self.nc, self.live, self.eng, self.ch = nc, live, eng, chans

    def wait(self, engine, ch, val):
        if isinstance(ch, DmaGrp):
            if self.live == engine:
                for s, c in zip(ch.subs, ch.watermark(val)):
                    if c > 0:
                        self.eng.wait_ge(s.sem, c)
            return
        if self.live == engine:
            self.eng.wait_ge(ch.sem, val)

    def inc(self, instr, ch, n):
        if instr is not None:
            instr.then_inc(ch.sem, n)
        ch.n += n

    def dma(self, engine, out, in_, ch, **kw):
        if isinstance(ch, DmaGrp):
            sub = ch.subs[(ch.n // 16) % len(ch.subs)]
            if self.live == engine:
                if sub.n > 0:
                    self.eng.wait_ge(sub.sem, sub.n)
                self.eng.dma_start(out=out, in_=in_, **kw).then_inc(sub.sem, 16)
            sub.n += 16
            ch.n += 16
            ch.hist.append(tuple(s.n for s in ch.subs))
            return
        i = self.eng.dma_start(out=out, in_=in_, **kw) if self.live == engine else None
        self.inc(i, ch, 16)

    def op(self, engine, fn, ch=None):
        i = fn(self.eng) if self.live == engine else None
        if ch is not None:
            self.inc(i, ch, 1)
        return i

    def bar(self):
        for nm in ("pe", "ve", "se", "ge", "st", "cw", "cwg", "cx", "cc"):
            ch = self.ch.get(nm)
            if ch is None:
                continue
            if isinstance(ch, DmaGrp):
                for s in ch.subs:
                    if s.n > 0:
                        self.eng.wait_ge(s.sem, s.n)
            elif ch.n > 0:
                self.eng.wait_ge(ch.sem, ch.n)


def build(stop_after=None, debug=(), paranoid=False, no_coll=False, qp_n=16, skip5=False, skip55=False):
    nc = bass.Bass("TRN2", target_bir_lowering=False, detect_race_conditions=False)
    ap = {}

    def inp(name, shape, dt=F32):
        ap[name] = nc.dram_tensor(name, list(shape), dt, kind="ExternalInput").ap()

    def outp(name, shape, dt=F32):
        ap[name] = nc.dram_tensor(name, list(shape), dt, kind="ExternalOutput").ap()

    def dram(name, shape, dt):
        ap[name] = nc.dram_tensor(name, list(shape), dt).ap()

    inp("x0", (S, D))
    inp("wqT", (D, D)); inp("wkT", (D, D)); inp("wvT", (D, D)); inp("woT", (D, D))
    inp("w1T", (D, FF)); inp("w2T", (FF, D))
    inp("l1qwT", (D, D)); inp("l1kwT", (D, D)); inp("l1vwT", (D, D)); inp("l1owT", (D, D))
    inp("l1qb", (128, 16)); inp("l1kb", (1, D)); inp("l1vb", (1, D)); inp("l1ob", (1, D))
    inp("l1lnw", (1, D)); inp("l1lnb", (1, D))
    inp("w1s", (FF // 8, D)); inp("l1b1", (1, FF))
    inp("w2s", (D // 8, FF)); inp("l1b2", (1, D))
    inp("cosT", (HD, S)); inp("sinTn", (HD, S))
    DBG = {"xnT": ((128, 32768), BF16), "s0": ((128, 16384), BF16),
           "aoT": ((128, 16384), BF16), "xqT": ((128, 16384), BF16),
           "cols": ((128, 64), F32), "crow": ((128, 1024), F32),
           "ub": ((FF, SOWN), F32), "n2": ((128, 32768), BF16),
           "qT": ((128, 1024), BF16), "kT": ((128, 2048), BF16),
           "wost": ((128, 2048), BF16),
           "v4": ((128, 16384), BF16), "zacc": ((128, 1024), F32),
           "xn1T": ((128, 16384), BF16), "kpv": ((128, 32768), BF16),
           "kvin": ((D + NH, HD), F32), "qpT": ((128, 16384), BF16),
           "attnT": ((128, 16384), BF16), "s1b": ((SOWN, D), BF16),
           "xmxT": ((128, 16384), BF16), "xmx2T": ((128, 32768), BF16)}
    for nm in debug:
        outp("dbg_" + nm, DBG[nm][0], DBG[nm][1])
    inp("ident", (128, 128), BF16)
    inp("identf", (128, 128), F32)
    inp("cmask", (2, 128, 128), BF16)
    outp("out", (SOWN, D))

    apt = {}

    def dramt(name, shape, dt):
        apt[name] = nc.dram_tensor(name, list(shape), dt)
        ap[name] = apt[name].ap()

    dram("u_b", (FF, SOWN), F32)
    dram("x1_b", (SOWN, D), F32)
    dram("w2q_b", (FF, D), BF16)
    dram("s1_b", (SOWN, D), BF16)
    dramt("kv_in", (D + NH, HD), F32)
    dramt("kv_out", (D + NH, HD), F32)
    dramt("ag1_in", (FF // 8, D), BF16)
    dramt("ag1_out", (FF, D), BF16)
    dramt("ag2_in", (D // 8, FF), BF16)
    dramt("ag2_out", (D, FF), BF16)

    ctxs = []

    def sbuf(name, shape, dt):
        cm = nc.sbuf_tensor(name, list(shape), dt)
        t = cm.__enter__(); ctxs.append(cm)
        return t

    def psum(name, shape, dt=F32):
        cm = nc.psum_tensor(name, list(shape), dt)
        t = cm.__enter__(); ctxs.append(cm)
        return t

    AB1 = sbuf("AB1", (128, 32768), BF16)
    AB2 = sbuf("AB2", (128, 16384), BF16)
    AB3 = sbuf("AB3", (128, 16384), BF16)
    AF2 = sbuf("AF2", (128, 5120), F32)
    WQB = [sbuf("WQB0", (128, 4096), BF16), sbuf("WQB1", (128, 4096), BF16)]
    KTB = sbuf("KTB", (128, S), BF16)
    VHB = sbuf("VHB", (128, S), BF16)
    EXPB = sbuf("EXPB", (128, 1024), BF16)
    ZACC = sbuf("ZACC", (128, SOWN), F32)
    MACC = sbuf("MACC", (128, SOWN), F32)
    CROW = sbuf("CROW", (128, 1024), F32)
    UB = [sbuf("UB0", (128, 1024), F32), sbuf("UB1", (128, 1024), F32)]
    XQI = sbuf("XQI", (128, 2048), I8)
    W1QB = sbuf("W1QB", (128, 2048), BF16)
    COLS = sbuf("COLS", (128, 64), F32)
    QBC = sbuf("QBC", (128, 16), F32)
    IDENT = sbuf("IDENT", (128, 128), BF16)
    IDENTF = sbuf("IDENTF", (128, 128), F32)
    CMASK = sbuf("CMASK", (128, 256), BF16)
    ONE1F = sbuf("ONE1F", (1, 128), F32)
    ONESB = sbuf("ONESB", (128, 128), BF16)
    ONECOL = sbuf("ONECOL", (128, 4), F32)
    ONECOLB = sbuf("ONECOLB", (128, 4), BF16)
    BROWF = sbuf("BROWF", (1, 512), F32)
    ZR = sbuf("ZR", (1, 512), F32)
    CR8 = sbuf("CR8", (16, 128), F32)

    PS_S = psum("PS_S", (128, 1024), F32)
    PS_ACC = psum("PS_ACC", (128, 1024), F32)
    PS_A = psum("PS_A", (128, 512), F32)
    PS_B = psum("PS_B", (128, 512), F32)
    PS_AUX = psum("PS_AUX", (128, 512), F32)
    PS_ROW = psum("PS_ROW", (1, 512), F32)
    PS_Sb = PS_S[:, :].bitcast(BF16)
    W1QB2 = CROW[:, :].bitcast(BF16)
    BAR2 = W1QB[:, 0:512]
    KP16 = CR8
    WQI = XQI
    QTB = VHB[:, 0:1024]

    xnT = AB1
    aoT = AB2
    xqT = AB2
    s0 = AB3
    cosT_s = AF2[:, 0:2048]
    sinTn_s = AF2[:, 2048:4096]
    xst = [AF2[:, 0:2048], AF2[:, 2048:4096]]
    sq2 = ZACC[:, 0:1024]
    WSTv = [AF2[:, 0:2048], AF2[:, 2048:4096]]
    XBFv = AF2[:, 0:4096].bitcast(BF16)

    chans = {}
    CHN = ["bar", "cx", "cw", "cwg", "pe", "ve", "se", "ge", "st", "cc"]

    def program(e: Emit):
        CH = e.ch
        V, SE, T, G, SY = "vector", "scalar", "tensor", "gpsimd", "sync"

        def dump(nm, src_ap):
            if nm in debug:
                e.wait(G, CH["ve"], CH["ve"].n)
                e.wait(G, CH["se"], CH["se"].n)
                n = src_ap.shape[1]
                step = 8192
                for o in range(0, n, step):
                    w = min(step, n - o)
                    e.dma(G, ap["dbg_" + nm][:, o:o + w], src_ap[:, o:o + w],
                          CH["st"])

        def TS(out, in0, s1, s2, op0, op1=None):
            return lambda v: v.tensor_scalar(
                out=out, in0=in0, scalar1=s1, scalar2=s2, op0=op0,
                op1=(op1 if op1 is not None else ALU.bypass))

        def TT(out, in0, in1, op):
            return lambda v: v.tensor_tensor(out=out, in0=in0, in1=in1, op=op)

        def MM(out, lhsT, rhs, start, stop):
            return lambda t: t.matmul(out, lhsT, rhs, start=start, stop=stop)

        def RED(out, in_, op, absval=False):
            return lambda v: v.tensor_reduce(out=out, in_=in_, axis=AXX, op=op,
                                             apply_absolute_value=absval)

        def VS():
            e.wait(V, CH["ve"], CH["ve"].n)

        def vtiny(fn):
            e.op(V, fn, CH["ve"])
            VS()

        def ACTF(out, in_, func, bias=None, scale=None):
            def f(s):
                kw = {}
                if bias is not None:
                    kw["bias"] = bias
                if scale is not None:
                    kw["scale"] = scale
                return s.activation(out=out, in_=in_, func=func, **kw)
            return f

        e.dma(SY, IDENT[:], ap["ident"], CH["cx"])
        e.dma(SY, IDENTF[:], ap["identf"], CH["cx"])
        e.dma(SY, CMASK[:, 0:128], ap["cmask"][0], CH["cx"])
        e.dma(SY, CMASK[:, 128:256], ap["cmask"][1], CH["cx"])
        e.dma(SY, QBC[:], ap["l1qb"], CH["cx"])
        for g_ in [lambda g: g.memset(COLS[:], 0.0),
                   lambda g: g.memset(ONE1F[:], 1.0),
                   lambda g: g.memset(ONESB[:], 1.0),
                   lambda g: g.memset(ONECOL[:], 1.0),
                   lambda g: g.memset(ONECOLB[:], 1.0),
                   lambda g: g.memset(BAR2[:], 0.0),
                   ]:
            e.op(G, g_, CH["ge"])
        e.bar()

        def ln_stats(xt_, cneg, cr):
            c0, c1, c2 = 32, 33, 34
            e.op(V, RED(COLS[:, c0:c0 + 1], xt_[:], ALU.add), CH["ve"])
            VS()
            vtiny(TS(COLS[:, cneg:cneg + 1], COLS[:, c0:c0 + 1], -1.0 / D, None,
                     ALU.mult))
            e.wait(SE, CH["ve"], CH["ve"].n)
            e.op(SE, ACTF(MACC[:], xt_[:, 0:1024], ACT.Square,
                          bias=COLS[:, cneg:cneg + 1]), CH["se"])
            e.op(SE, ACTF(sq2[:], xt_[:, 1024:2048], ACT.Square,
                          bias=COLS[:, cneg:cneg + 1]), CH["se"])
            e.wait(V, CH["se"], CH["se"].n)
            e.op(V, RED(COLS[:, c0:c0 + 1], MACC[:], ALU.add), CH["ve"])
            e.op(V, RED(COLS[:, c1:c1 + 1], sq2[:], ALU.add), CH["ve"])
            VS()
            vtiny(TT(COLS[:, c0:c0 + 1], COLS[:, c0:c0 + 1], COLS[:, c1:c1 + 1],
                     ALU.add))
            vtiny(TS(COLS[:, c0:c0 + 1], COLS[:, c0:c0 + 1], 1.0 / D, 1e-5,
                     ALU.mult, ALU.add))
            e.wait(SE, CH["ve"], CH["ve"].n)
            e.op(SE, ACTF(COLS[:, c1:c1 + 1], COLS[:, c0:c0 + 1], ACT.Sqrt), CH["se"])
            e.wait(V, CH["se"], CH["se"].n)
            if paranoid and cneg < 8:
                e.op(V, lambda v: v.tensor_copy(COLS[:, 52 + cneg:53 + cneg],
                                                COLS[:, c0:c0 + 1]), CH["ve"])
                e.op(V, lambda v: v.tensor_copy(COLS[:, 44 + cneg:45 + cneg],
                                                COLS[:, c1:c1 + 1]), CH["ve"])
            vtiny(lambda v: v.reciprocal(COLS[:, cr:cr + 1], COLS[:, c1:c1 + 1]))
            vtiny(TT(COLS[:, c2:c2 + 1], COLS[:, cr:cr + 1], COLS[:, cr:cr + 1],
                     ALU.mult))
            vtiny(TT(COLS[:, c2:c2 + 1], COLS[:, c2:c2 + 1], COLS[:, c0:c0 + 1],
                     ALU.mult))
            vtiny(TS(COLS[:, c2:c2 + 1], COLS[:, c2:c2 + 1], -0.5, 1.5,
                     ALU.mult, ALU.add))
            vtiny(TT(COLS[:, cr:cr + 1], COLS[:, cr:cr + 1], COLS[:, c2:c2 + 1],
                     ALU.mult))

        def transpose16(src_bf, dst, dst_tile_cols, col_off, wait_ve, src_off=0):
            for half in range(2):
                e.wait(T, CH["se"], CH["se"].n)
                e.wait(T, CH["ve"], wait_ve)
                off = 1024 * half
                for j in range(8):
                    k = 8 * half + j
                    e.op(T, lambda t, k=k, j=j, off=off: t.transpose(
                        PS_Sb[:, off + 128 * j:off + 128 * (j + 1)],
                        src_bf[:, src_off + 128 * k:src_off + 128 * (k + 1)],
                        IDENT[:]), CH["pe"])
                e.wait(SE, CH["pe"], CH["pe"].n)
                e.op(SE, lambda s, half=half, off=off: s.copy(
                    dst[:, :].rearrange("p (k c) -> p k c", c=dst_tile_cols)[
                        :, 8 * half:8 * half + 8, col_off:col_off + 128],
                    PS_Sb[:, off:off + 1024].rearrange("p (j c) -> p j c", j=8)),
                    CH["se"])
            return CH["pe"].n

        applied = {}
        stage_pe = {}
        for t in range(NTF):
            xt_ = xst[t % 2]
            if t >= 2:
                e.wait(SY, CH["ve"], applied[t - 2])
            e.dma(SY, xt_[:], ap["x0"][128 * t:128 * (t + 1), :], CH["cx"])
            e.wait(V, CH["cx"], CH["cx"].n)
            own = t < 8
            cneg = t if own else 48
            cr = (8 + t) if own else 49
            ln_stats(xt_, cneg, cr)
            stage = KTB if t % 2 == 0 else VHB
            if t >= 2:
                e.wait(V, CH["pe"], stage_pe[t - 2])
            e.op(V, TS(stage[:], xt_[:], COLS[:, cneg:cneg + 1],
                       COLS[:, cr:cr + 1], ALU.add, ALU.mult), CH["ve"])
            ve_stage = CH["ve"].n
            if own:
                e.op(V, TS(MACC[:], xt_[:, 0:1024], COLS[:, cneg:cneg + 1],
                           COLS[:, cr:cr + 1], ALU.add, ALU.mult), CH["ve"])
                e.op(V, TS(sq2[:], xt_[:, 1024:2048], COLS[:, cneg:cneg + 1],
                           COLS[:, cr:cr + 1], ALU.add, ALU.mult), CH["ve"])
                e.op(V, RED(COLS[:, 36:37], MACC[:], ALU.max, True), CH["ve"])
                e.op(V, RED(COLS[:, 37:38], sq2[:], ALU.max, True), CH["ve"])
                VS()
                vtiny(TT(COLS[:, 36:37], COLS[:, 36:37], COLS[:, 37:38],
                         ALU.max))
                vtiny(TS(COLS[:, 16 + t:17 + t], COLS[:, 36:37], 1e-5, None,
                         ALU.max))
                vtiny(lambda v, t=t: v.reciprocal(COLS[:, 38:39],
                                                  COLS[:, 16 + t:17 + t]))
                vtiny(TS(COLS[:, 24 + t:25 + t], COLS[:, 38:39], 127.0, None,
                         ALU.mult))
            applied[t] = CH["ve"].n
            if paranoid:
                e.bar()
            stage_pe[t] = transpose16(stage, xnT, S, 128 * t, ve_stage)
            if paranoid:
                e.bar()

        dump("xnT", xnT[:, :])
        dump("cols", COLS[:, :])
        if stop_after == "ph1":
            e.bar()
            e.wait(G, CH["st"], CH["st"].n)
            return
        stat_ve = {}
        ntiles = {"w1T": 64, "w2T": 64}
        idx = 0
        for wi, wname in ((0, "w1T"), (1, "w2T")):
            n_r = ap[wname].shape[0] // 128
            n_c = ap[wname].shape[1] // 2048
            for j in range(n_r * n_c):
                buf = WSTv[idx % 2]
                if idx >= 2:
                    e.wait(SY, CH["ve"], stat_ve[idx - 2])
                else:
                    e.wait(SY, CH["ve"], applied[14 + idx])
                rr, cc = j % n_r, j // n_r
                e.dma(SY, buf[:], ap[wname][128 * rr:128 * (rr + 1),
                                            2048 * cc:2048 * (cc + 1)], CH["cw"])
                e.wait(V, CH["cw"], CH["cw"].n)
                e.op(V, RED(COLS[:, 36:37], buf[:], ALU.add, True), CH["ve"])
                VS()
                vtiny(TT(COLS[:, 40 + wi:41 + wi], COLS[:, 40 + wi:41 + wi],
                         COLS[:, 36:37], ALU.add))
                stat_ve[idx] = CH["ve"].n
                idx += 1
        e.wait(T, CH["ve"], CH["ve"].n)
        e.op(T, MM(PS_ROW[:, 0:2], ONECOL[:, 0:1], COLS[:, 40:42], True, True),
             CH["pe"])
        e.wait(V, CH["pe"], CH["pe"].n)
        vtiny(TS(ZR[:, 0:2], PS_ROW[:, 0:2], 1.0 / (D * FF), 1e-5,
                 ALU.mult, ALU.max))
        vtiny(lambda v: v.reciprocal(ZR[:, 2:4], ZR[:, 0:2]))
        e.wait(T, CH["ve"], CH["ve"].n)
        e.op(T, MM(PS_AUX[:, 0:4], ONE1F[:], ZR[:, 0:4], True, True), CH["pe"])
        e.wait(V, CH["pe"], CH["pe"].n)
        vtiny(lambda v: v.tensor_copy(COLS[:, 46:48], PS_AUX[:, 0:2]))
        vtiny(lambda v: v.tensor_copy(COLS[:, 44:46], PS_AUX[:, 2:4]))
        e.bar()
        if stop_after == "ph1c":
            dump("cols", COLS[:, :])
            e.wait(G, CH["st"], CH["st"].n)
            return

        e.wait(SY, CH["ve"], CH["ve"].n)
        e.dma(SY, cosT_s[:], ap["cosT"], CH["cx"])
        e.dma(SY, sinTn_s[:], ap["sinTn"], CH["cx"])
        tables_cx = CH["cx"].n

        def load_head_w(h, buf):
            for wn, off in (("wqT", 0), ("wkT", 2048)):
                e.dma(G, buf[:, off:off + 2048].rearrange("p (k c) -> p k c", k=16),
                      ap[wn].rearrange("(k p) c -> p k c", p=128)[
                          :, :, 128 * h:128 * (h + 1)], CH["cwg"])

        load_head_w(0, WQB[0])
        cw_head = {0: CH["cwg"].n}
        rope_done = {}
        attn_evac = {}
        for h in range(NH):
            grp, buf = h // 4, WQB[h % 2]
            if h % 4 == 0:
                if h > 0:
                    e.wait(G, CH["pe"], v4_build_pe)
                e.dma(G, AB3[:, 8192:16384].rearrange("p (k c) -> p k c", k=16),
                      ap["wvT"].rearrange("(k p) c -> p k c", p=128)[
                          :, :, 512 * grp:512 * (grp + 1)], CH["cwg"])
                grp_cw = CH["cwg"].n
            if h + 1 < NH:
                load_head_w(h + 1, WQB[(h + 1) % 2])
                cw_head[h + 1] = CH["cwg"].n
            if h % 4 == 0:
                e.wait(T, CH["cwg"], grp_cw)
                if h > 0:
                    e.wait(T, CH["ve"], attn_evac[h - 1])
                v4_se = {}
                for g_t in range(16):
                    pss = (PS_A, PS_B)[g_t % 2]
                    if g_t >= 2:
                        e.wait(T, CH["se"], v4_se[g_t - 2])
                    for k in range(16):
                        e.op(T, MM(pss[:],
                                   xnT[:, k * S + 128 * g_t:k * S + 128 * (g_t + 1)],
                                   AB3[:, 8192 + 512 * k:8192 + 512 * (k + 1)],
                                   k == 0, k == 15), CH["pe"])
                    e.wait(SE, CH["pe"], CH["pe"].n)
                    e.op(SE, lambda s, g_t=g_t, pss=pss: s.copy(
                        AB3[:, 512 * g_t:512 * (g_t + 1)], pss[:]), CH["se"])
                    v4_se[g_t] = CH["se"].n
                v4_build_pe = CH["pe"].n
            e.wait(T, CH["cwg"], cw_head[h])
            if h > 0:
                e.wait(T, CH["ve"], rope_done[h - 1])
                e.wait(T, CH["ve"], attn_evac[h - 1])
            for c in range(2):
                for k in range(16):
                    e.op(T, MM(PS_ACC[:, 512 * c:512 * (c + 1)],
                               buf[:, 128 * k:128 * (k + 1)],
                               xnT[:, k * S + 512 * c:k * S + 512 * (c + 1)],
                               k == 0, k == 15), CH["pe"])
            e.wait(V, CH["pe"], CH["pe"].n)
            e.wait(V, CH["cx"], tables_cx)
            e.op(V, TT(MACC[:], PS_ACC[:], cosT_s[:, 0:1024], ALU.mult), CH["ve"])
            e.op(V, TT(ZACC[0:64, :], PS_ACC[64:128, :], sinTn_s[0:64, 0:1024],
                       ALU.mult), CH["ve"])
            e.op(V, TT(ZACC[64:128, :], PS_ACC[0:64, :], sinTn_s[64:128, 0:1024],
                       ALU.mult), CH["ve"])
            e.op(V, TT(QTB[:], MACC[:], ZACC[:], ALU.add), CH["ve"])
            rope_q = CH["ve"].n
            for r in range(2):
                e.wait(T, CH["ve"], CH["ve"].n)
                for c in range(2):
                    for k in range(16):
                        e.op(T, MM(PS_ACC[:, 512 * c:512 * (c + 1)],
                                   buf[:, 2048 + 128 * k:2048 + 128 * (k + 1)],
                                   xnT[:, k * S + 1024 * r + 512 * c:
                                       k * S + 1024 * r + 512 * (c + 1)],
                                   k == 0, k == 15), CH["pe"])
                e.wait(V, CH["pe"], CH["pe"].n)
                sl = slice(1024 * r, 1024 * (r + 1))
                e.op(V, TT(MACC[:], PS_ACC[:], cosT_s[:, sl], ALU.mult), CH["ve"])
                e.op(V, TT(ZACC[0:64, :], PS_ACC[64:128, :], sinTn_s[0:64, sl],
                           ALU.mult), CH["ve"])
                e.op(V, TT(ZACC[64:128, :], PS_ACC[0:64, :], sinTn_s[64:128, sl],
                           ALU.mult), CH["ve"])
                e.op(V, TT(KTB[:, sl], MACC[:], ZACC[:], ALU.add), CH["ve"])
            rope_done[h] = CH["ve"].n
            e.wait(G, CH["ve"], rope_done[h])
            e.op(G, lambda g: g.memset(ZACC[:], 0.0), CH["ge"])
            ge_z0 = CH["ge"].n
            e.wait(T, CH["ve"], rope_done[h])
            e.wait(T, CH["ge"], ge_z0)
            zadd_ve = 0
            for kp in range(16):
                j = kp % 8
                qs = 128 * j
                e.wait(T, CH["se"], CH["se"].n)
                e.wait(T, CH["ve"], zadd_ve)
                for c in (0, 512):
                    lo = max(qs, c)
                    if lo >= c + 512:
                        continue
                    e.op(T, MM(PS_S[:, lo:c + 512], KTB[:, 128 * kp:128 * (kp + 1)],
                               QTB[:, lo:c + 512], True, True), CH["pe"])
                e.wait(SE, CH["pe"], CH["pe"].n)
                e.op(SE, ACTF(EXPB[:, 0:1024 - qs], PS_S[:, qs:1024], ACT.Exp,
                              scale=ISQ), CH["se"])
                exp_se = CH["se"].n
                e.wait(V, CH["se"], exp_se)
                m = 0 if kp < 8 else 1
                e.op(V, TT(EXPB[:, 0:128], EXPB[:, 0:128],
                           CMASK[:, 128 * m:128 * (m + 1)], ALU.mult), CH["ve"])
                e.op(V, TT(ZACC[:, qs:1024], ZACC[:, qs:1024],
                           EXPB[:, 0:1024 - qs], ALU.add), CH["ve"])
                zadd_ve = CH["ve"].n
                e.wait(T, CH["ve"], zadd_ve)
                for c in (0, 512):
                    lo = max(qs, c)
                    if lo >= c + 512:
                        continue
                    start = (kp == 0)
                    stop = (kp == 11) if c == 0 else (kp == 15)
                    e.op(T, MM(PS_ACC[:, lo:c + 512],
                               AB3[:, 512 * kp + 128 * (h % 4):
                                   512 * kp + 128 * (h % 4) + 128],
                               EXPB[:, lo - qs:c + 512 - qs], start, stop),
                         CH["pe"])
            pe_av = CH["pe"].n
            for c in range(2):
                e.wait(T, CH["ve"], CH["ve"].n)
                e.op(T, MM(PS_ROW[:, 0:512], ONECOL[:, 0:1],
                           ZACC[:, 512 * c:512 * (c + 1)], True, True), CH["pe"])
                e.wait(V, CH["pe"], CH["pe"].n)
                e.op(V, lambda v, c=c: v.reciprocal(ZR[:, 0:512],
                                                    PS_ROW[:, 0:512]), CH["ve"])
                e.wait(T, CH["ve"], CH["ve"].n)
                e.op(T, MM(PS_AUX[:], ONE1F[:], ZR[:, 0:512],
                           True, True), CH["pe"])
                e.wait(V, CH["pe"], CH["pe"].n)
                e.op(V, lambda v, c=c: v.tensor_copy(MACC[:, 512 * c:512 * (c + 1)],
                                                     PS_AUX[:]), CH["ve"])
                e.op(V, TT(aoT[:, SOWN * h + 512 * c:SOWN * h + 512 * (c + 1)],
                           PS_ACC[:, 512 * c:512 * (c + 1)],
                           MACC[:, 512 * c:512 * (c + 1)], ALU.mult),
                     CH["ve"])
            attn_evac[h] = CH["ve"].n
        CROWb = CROW[:, :].bitcast(BF16)
        wo_stage = [(CROWb, 0), (CROWb, 1), (CROWb, 2), (CROWb, 3),
                    (KTB, 0), (KTB, 1), (KTB, 2), (KTB, 3),
                    (VHB, 0), (VHB, 1), (VHB, 2), (VHB, 3),
                    (W1QB, 0), (W1QB, 1), (EXPB, 0), (EXPB, 1)]

        def wo_rhs(k):
            tns, sl = wo_stage[k]
            return tns[:, 512 * sl:512 * (sl + 1)]

        for dc in range(4):
            e.wait(G, CH["ve"], attn_evac[NH - 1])
            e.wait(T, CH["ve"], attn_evac[NH - 1])
            if dc > 0:
                e.wait(G, CH["pe"], CH["pe"].n)
            for tns, k0, nk in ((CROWb, 0, 4), (KTB, 4, 4), (VHB, 8, 4),
                                (W1QB, 12, 2), (EXPB, 14, 2)):
                e.dma(G, tns[:, 0:512 * nk].rearrange("p (k c) -> p k c", k=nk),
                      ap["woT"].rearrange("(k p) c -> p k c", p=128)[
                          :, k0:k0 + nk, 512 * dc:512 * (dc + 1)],
                      CH["cwg"])
            e.wait(T, CH["cwg"], CH["cwg"].n)
            for tt_ in range(8):
                pss = (PS_A, PS_B)[tt_ % 2]
                e.wait(T, CH["se"], CH["se"].n)
                for k in range(16):
                    e.op(T, MM(pss[:],
                               aoT[:, SOWN * k + 128 * tt_:SOWN * k + 128 * (tt_ + 1)],
                               wo_rhs(k), k == 0, k == 15), CH["pe"])
                e.wait(SE, CH["pe"], CH["pe"].n)
                e.op(SE, lambda s, tt_=tt_, dc=dc, pss=pss: s.copy(
                    s0[:, D * tt_ + 512 * dc:D * tt_ + 512 * (dc + 1)], pss[:]),
                    CH["se"])
        dump("aoT", aoT[:, :])
        dump("s0", s0[:, :])
        if stop_after == "attn":
            e.wait(G, CH["st"], CH["st"].n)
            return
        dump("qT", QTB[:, :])
        dump("kT", KTB[:, :])
        dump("v4", AB3[:, 0:8192])
        dump("zacc", ZACC[:, :])
        e.bar()

        for t in range(NT):
            xt_ = xst[t % 2]
            if t >= 2:
                e.wait(SY, CH["ve"], xq_ve[t - 2])
            e.dma(SY, xt_[:], ap["x0"][128 * t:128 * (t + 1), :], CH["cx"])
            e.wait(V, CH["cx"], CH["cx"].n)
            e.op(V, TS(MACC[:], xt_[:, 0:1024], COLS[:, t:t + 1],
                       COLS[:, 8 + t:9 + t], ALU.add, ALU.mult), CH["ve"])
            e.op(V, TS(sq2[:], xt_[:, 1024:2048], COLS[:, t:t + 1],
                       COLS[:, 8 + t:9 + t], ALU.add, ALU.mult), CH["ve"])
            e.op(V, TS(XQI[:, 0:1024], MACC[:], COLS[:, 24 + t:25 + t], None,
                       ALU.mult), CH["ve"])
            e.op(V, TS(XQI[:, 1024:2048], sq2[:], COLS[:, 24 + t:25 + t], None,
                       ALU.mult), CH["ve"])
            if t >= 1:
                e.wait(V, CH["pe"], xq_pe[t - 1])
            e.op(V, lambda v: v.tensor_copy(VHB[:], XQI[:]), CH["ve"])
            if t == 0:
                xq_ve = {}
                xq_pe = {}
            xq_ve[t] = CH["ve"].n
            xq_pe[t] = transpose16(VHB, xqT, SOWN, 128 * t, xq_ve[t])
        dump("xqT", xqT[:, :])
        for j in range(64):
            buf = WSTv[j % 2]
            if j >= 2:
                e.wait(SY, CH["st"], w2q_st[j - 2])
                e.wait(SY, CH["ve"], w2q_ve2[j - 2])
            else:
                e.wait(SY, CH["ve"], xq_ve[6 + j])
            e.dma(SY, buf[:], ap["w2T"][128 * j:128 * (j + 1), :], CH["cw"])
            e.wait(V, CH["cw"], CH["cw"].n)
            if j >= 2:
                e.wait(V, CH["st"], w2q_st[j - 2])
            if j >= 1:
                e.wait(V, CH["st"], w2q_st[j - 1])
            e.op(V, TS(WQI[:], buf[:], COLS[:, 45:46], None, ALU.mult), CH["ve"])
            e.op(V, TS(W1QB[:], WQI[:], 1.0, -1.0, ALU.min, ALU.max), CH["ve"])
            if j == 0:
                w2q_st = {}
                w2q_ve2 = {}
            w2q_ve2[j] = CH["ve"].n
            e.wait(G, CH["ve"], CH["ve"].n)
            e.dma(G, ap["w2q_b"][128 * j:128 * (j + 1), :], W1QB[:], CH["st"])
            w2q_st[j] = CH["st"].n
        e.op(G, lambda g: g.memset(MACC[:], 0.0), CH["ge"])
        ge_m0 = CH["ge"].n
        mm1_se = {}
        mm1_st = {}
        mm1_wq = {}
        mm1_pe = {}
        for fc in range(64):
            buf = WSTv[fc % 2]
            if fc >= 2:
                e.wait(SY, CH["ve"], mm1_wq[fc - 2])
            else:
                e.wait(SY, CH["ve"], w2q_ve2[62 + fc])
            e.dma(SY, buf[:].rearrange("p (k c) -> p k c", k=16),
                  ap["w1T"].rearrange("(k p) c -> p k c", p=128)[
                      :, :, 128 * fc:128 * (fc + 1)], CH["cw"])
            e.wait(V, CH["cw"], CH["cw"].n)
            if fc >= 2:
                e.wait(V, CH["pe"], mm1_pe[fc - 2])
            wq = (W1QB, W1QB2)[fc % 2]
            if fc == 0:
                e.wait(V, CH["st"], w2q_st[63])
            e.op(V, TS(WQI[:], buf[:], COLS[:, 44:45], None, ALU.mult), CH["ve"])
            e.op(V, TS(wq[:], WQI[:], 1.0, -1.0, ALU.min, ALU.max), CH["ve"])
            mm1_wq[fc] = CH["ve"].n
            e.wait(T, CH["ve"], mm1_wq[fc])
            for c in range(2):
                pss = (PS_A, PS_B)[c]
                if fc >= 1:
                    e.wait(T, CH["se"], mm1_se[(fc - 1, c)])
                for k in range(16):
                    e.op(T, MM(pss[:], wq[:, 128 * k:128 * (k + 1)],
                               xqT[:, SOWN * k + 512 * c:SOWN * k + 512 * (c + 1)],
                               k == 0, k == 15), CH["pe"])
                mm1_pe[fc] = CH["pe"].n
                e.wait(SE, CH["pe"], CH["pe"].n)
                e.wait(SE, CH["st"], mm1_st.get((fc - 2, c), 0))
                ub = UB[c]
                e.op(SE, ACTF(ub[:, 0:512], pss[:], ACT.Relu), CH["se"])
                mm1_se[(fc, c)] = CH["se"].n
                e.wait(V, CH["se"], mm1_se[(fc, c)])
                e.wait(V, CH["ge"], ge_m0)
                e.op(V, TT(MACC[:, 512 * c:512 * (c + 1)],
                           MACC[:, 512 * c:512 * (c + 1)], ub[:, 0:512], ALU.max),
                     CH["ve"])
                e.wait(G, CH["ve"], CH["ve"].n)
                e.dma(G, ap["u_b"][128 * fc:128 * (fc + 1), 512 * c:512 * (c + 1)],
                      ub[:, 0:512], CH["st"])
                mm1_st[(fc, c)] = CH["st"].n
        for a in range(8):
            e.wait(T, CH["ve"], CH["ve"].n)
            e.op(T, lambda t, a=a: t.transpose(PS_AUX[:, 0:128],
                                               MACC[:, 128 * a:128 * (a + 1)],
                                               IDENTF[:]), CH["pe"])
            e.wait(V, CH["pe"], CH["pe"].n)
            e.op(V, RED(COLS[:, 32:33], PS_AUX[:, 0:128], ALU.max), CH["ve"])
            VS()
            vtiny(TS(COLS[:, 33:34], COLS[:, 16 + a:17 + a], 1.0 / 127.0, None,
                     ALU.mult))
            vtiny(TT(COLS[:, 33:34], COLS[:, 33:34], COLS[:, 46:47], ALU.mult))
            vtiny(TT(COLS[:, 34:35], COLS[:, 32:33], COLS[:, 33:34], ALU.mult))
            vtiny(TT(COLS[:, 34:35], COLS[:, 34:35], COLS[:, 34:35], ALU.mult))
            vtiny(TS(COLS[:, 35:36], COLS[:, 34:35], 1e-5, None, ALU.max))
            vtiny(lambda v: v.reciprocal(COLS[:, 36:37], COLS[:, 35:36]))
            vtiny(TS(COLS[:, 37:38], COLS[:, 36:37], 127.0, None, ALU.mult))
            vtiny(TS(COLS[:, 52 + a:53 + a], COLS[:, 35:36], 1.0 / 127.0, None,
                     ALU.mult))
            vtiny(TT(COLS[:, 52 + a:53 + a], COLS[:, 52 + a:53 + a],
                     COLS[:, 47:48], ALU.mult))
            e.wait(SE, CH["ve"], CH["ve"].n)
            e.op(SE, ACTF(COLS[:, 38:39], COLS[:, 37:38], ACT.Sqrt), CH["se"])
            e.wait(V, CH["se"], CH["se"].n)
            vtiny(TT(COLS[:, 39:40], COLS[:, 33:34], COLS[:, 38:39], ALU.mult))
            e.wait(T, CH["ve"], CH["ve"].n)
            e.op(T, lambda t: t.transpose(PS_AUX[0:1, 128:256], COLS[:, 39:40],
                                          IDENTF[:]), CH["pe"])
            e.wait(V, CH["pe"], CH["pe"].n)
            e.op(V, lambda v: v.tensor_copy(CR8[0:1, :], PS_AUX[0:1, 128:256]),
                 CH["ve"])
            e.wait(T, CH["ve"], CH["ve"].n)
            e.op(T, MM(PS_AUX[:, 256:384], ONE1F[:], CR8[0:1, :], True, True),
                 CH["pe"])
            e.wait(V, CH["pe"], CH["pe"].n)
            e.op(V, lambda v, a=a: v.tensor_copy(CROW[:, 128 * a:128 * (a + 1)],
                                                 PS_AUX[:, 256:384]), CH["ve"])
        ve_crow = CH["ve"].n
        dump("crow", CROW[:, :])
        if "ub" in debug:
            e.wait(G, CH["st"], CH["st"].n)
            for f in range(64):
                e.dma(G, ap["dbg_ub"][128 * f:128 * (f + 1), :],
                      ap["u_b"][128 * f:128 * (f + 1), :], CH["st"])
        mm2_groups = [PS_A[:, :], PS_B[:, :], PS_ACC[:, 0:512], PS_ACC[:, 512:1024]]
        for ht in range(2):
            tc = slice(512 * ht, 512 * (ht + 1))
            nq_ve = {}
            for f in range(64):
                ub = UB[f % 2]
                if f >= 2:
                    e.wait(SY, CH["ve"], nq_ve[f - 2])
                elif ht == 0:
                    e.wait(SY, CH["st"], mm1_st[(63, f)])
                e.dma(SY, ub[:, 0:512], ap["u_b"][128 * f:128 * (f + 1), tc],
                      CH["cx"])
                e.wait(V, CH["cx"], CH["cx"].n)
                if ht == 0 and f == 0:
                    e.wait(V, CH["ve"], ve_crow)
                e.op(V, TT(ub[:, 512:1024], ub[:, 0:512], CROW[:, tc], ALU.mult),
                     CH["ve"])
                e.op(V, TT(XQI[:, 0:512], ub[:, 512:1024], ub[:, 512:1024],
                           ALU.mult), CH["ve"])
                e.op(V, lambda v, f=f: v.tensor_copy(
                    AB1[:, 512 * f:512 * (f + 1)], XQI[:, 0:512]), CH["ve"])
                nq_ve[f] = CH["ve"].n
            ve_nq = CH["ve"].n
            if ht == 0:
                dump("n2", AB1[:, :])
            for dc in range(4):
                if ht > 0 or dc > 0:
                    e.wait(SY, CH["ve"], gate_ve_last)
                for fb in range(16):
                    buf = (W1QB, KTB)[fb % 2]
                    if fb >= 2:
                        e.wait(SY, CH["pe"], mm2_pe[fb - 2])
                    elif ht == 0 and dc == 0:
                        e.wait(SY, CH["pe"], mm1_pe[63])
                    e.dma(SY, buf[:].rearrange("p (a c) -> p a c", a=4),
                          ap["w2q_b"].rearrange("(a p) c -> p a c", p=128)[
                              :, 4 * fb:4 * fb + 4, 512 * dc:512 * (dc + 1)],
                          CH["cw"])
                    e.wait(T, CH["cw"], CH["cw"].n)
                    if fb == 0:
                        e.wait(T, CH["ve"], ve_nq)
                        if ht > 0 or dc > 0:
                            e.wait(T, CH["ve"], gate_ve_last)
                    for ff in range(4):
                        f = 4 * fb + ff
                        for ts in range(4):
                            e.op(T, MM(mm2_groups[ts],
                                       AB1[:, 512 * f + 128 * ts:
                                           512 * f + 128 * (ts + 1)],
                                       buf[:, 512 * ff:512 * (ff + 1)],
                                       f == 0, f == 63), CH["pe"])
                    mm2_pe = mm2_pe if (fb > 0 or dc > 0 or ht > 0) else {}
                    mm2_pe[fb] = CH["pe"].n
                pe_mm2 = CH["pe"].n
                for ts in range(4):
                    a = 4 * ht + ts
                    e.wait(SE, CH["pe"], pe_mm2)
                    e.wait(SE, CH["ve"], ve_crow)
                    if ht > 0 or dc > 0 or ts > 0:
                        e.wait(SE, CH["ve"], gate_ve_last)
                    e.op(SE, lambda s, ts=ts, a=a: s.activation(
                        out=MACC[:, 0:512], in_=mm2_groups[ts], func=ACT.Silu,
                        scale=COLS[:, 52 + a:53 + a]), CH["se"])
                    se_g0 = CH["se"].n
                    if ht > 0 or dc > 0 or ts > 0:
                        e.wait(SY, CH["ve"], gate_ve_last)
                    e.dma(SY, ZACC[:, 0:512],
                          ap["x0"][128 * a:128 * (a + 1), 512 * dc:512 * (dc + 1)],
                          CH["cx"])
                    cx_x0 = CH["cx"].n
                    e.wait(V, CH["se"], se_g0)
                    e.wait(V, CH["cx"], cx_x0)
                    if ht > 0 or dc > 0 or ts > 0:
                        e.wait(V, CH["st"], st_gate_last)
                    e.op(V, TT(MACC[:, 512:1024],
                               s0[:, D * a + 512 * dc:D * a + 512 * (dc + 1)],
                               MACC[:, 0:512], ALU.mult), CH["ve"])
                    e.op(V, TT(ZACC[:, 512:1024], ZACC[:, 0:512],
                               MACC[:, 512:1024], ALU.add), CH["ve"])
                    ve_x1 = CH["ve"].n
                    e.wait(G, CH["ve"], ve_x1)
                    e.dma(G, ap["x1_b"][128 * a:128 * (a + 1),
                                        512 * dc:512 * (dc + 1)],
                          ZACC[:, 512:1024], CH["st"])
                    if stop_after == "x1":
                        e.dma(G, ap["out"][128 * a:128 * (a + 1),
                                           512 * dc:512 * (dc + 1)],
                              ZACC[:, 512:1024], CH["st"])
                    st_gate_last = CH["st"].n
                    gate_ve_last = CH["ve"].n
        e.bar()
        if stop_after == "x1":
            e.wait(G, CH["st"], CH["st"].n)
            return

        ln1_app = {}
        ln1_pe = {}
        for t in range(NT):
            xt_ = xst[t % 2]
            if t >= 2:
                e.wait(SY, CH["ve"], ln1_app[t - 2])
            e.dma(SY, xt_[:], ap["x1_b"][128 * t:128 * (t + 1), :], CH["cx"])
            e.wait(V, CH["cx"], CH["cx"].n)
            ln_stats(xt_, t, 8 + t)
            stage = KTB if t % 2 == 0 else VHB
            if t >= 2:
                e.wait(V, CH["pe"], ln1_pe[t - 2])
            e.op(V, TS(stage[:], xt_[:], COLS[:, t:t + 1],
                       COLS[:, 8 + t:9 + t], ALU.add, ALU.mult), CH["ve"])
            ln1_app[t] = CH["ve"].n
            ln1_pe[t] = transpose16(stage, AB2, 1024, 128 * t, ln1_app[t])
        dump("xn1T", AB2[:, :])
        if stop_after == "xn1T":
            e.bar()
            e.wait(G, CH["st"], CH["st"].n)
            return

        e.bar()
        kv_evac_ve = 0
        for dc in ([] if skip5 else range(4)):
            if dc > 0:
                e.wait(G, CH["pe"], CH["pe"].n)
            e.dma(G, AB3[:, 0:8192].rearrange("p (k c) -> p k c", k=16),
                  ap["l1kwT"].rearrange("(k p) c -> p k c", p=128)[
                      :, :, 512 * dc:512 * (dc + 1)], CH["cwg"])
            e.dma(G, AB3[:, 8192:16384].rearrange("p (k c) -> p k c", k=16),
                  ap["l1vwT"].rearrange("(k p) c -> p k c", p=128)[
                      :, :, 512 * dc:512 * (dc + 1)], CH["cwg"])
            cw_w = CH["cwg"].n
            for bn, dst in (("l1kb", UB[0]), ("l1vb", UB[1])):
                e.wait(SY, CH["pe"], CH["pe"].n)
                e.dma(SY, BROWF[:], ap[bn][0:1, 512 * dc:512 * (dc + 1)],
                      CH["cx"])
                e.wait(T, CH["cx"], CH["cx"].n)
                e.wait(T, CH["ve"], CH["ve"].n)
                e.op(T, MM(PS_AUX[:, 0:512], ONE1F[:], BROWF[:], True, True),
                     CH["pe"])
                e.wait(V, CH["pe"], CH["pe"].n)
                e.op(V, lambda v, dst=dst: v.tensor_copy(dst[:, 0:512],
                                                         PS_AUX[:, 0:512]),
                     CH["ve"])
            bias_ve = CH["ve"].n
            for tt in range(8):
                e.wait(T, CH["cwg"], cw_w)
                e.wait(T, CH["ve"], kv_evac_ve)
                e.wait(T, CH["ve"], bias_ve)
                for k in range(16):
                    e.op(T, MM(PS_A[:], AB2[:, 1024 * k + 128 * tt:
                                            1024 * k + 128 * (tt + 1)],
                               AB3[:, 512 * k:512 * (k + 1)],
                               k == 0, k == 15), CH["pe"])
                for k in range(16):
                    e.op(T, MM(PS_B[:], AB2[:, 1024 * k + 128 * tt:
                                            1024 * k + 128 * (tt + 1)],
                               AB3[:, 8192 + 512 * k:8192 + 512 * (k + 1)],
                               k == 0, k == 15), CH["pe"])
                pe_kv = CH["pe"].n
                e.wait(V, CH["pe"], pe_kv)
                e.op(V, TT(AB1[:, 16384 + 2048 * tt + 512 * dc:
                               16384 + 2048 * tt + 512 * (dc + 1)],
                           PS_B[:], UB[1][:, 0:512], ALU.add), CH["ve"])
                e.op(V, TT(MACC[:, 0:512], PS_A[:], UB[0][:, 0:512], ALU.add),
                     CH["ve"])
                ve_z = CH["ve"].n
                e.wait(SE, CH["ve"], ve_z)
                e.op(SE, ACTF(ZACC[:, 0:512], MACC[:, 0:512], ACT.Relu),
                     CH["se"])
                e.op(V, TS(MACC[:, 512:1024], MACC[:, 0:512], 0.0, None,
                           ALU.min), CH["ve"])
                ve_m = CH["ve"].n
                e.wait(SE, CH["ve"], ve_m)
                e.op(SE, ACTF(CROW[:, 0:512], MACC[:, 512:1024], ACT.Exp),
                     CH["se"])
                e.wait(V, CH["se"], CH["se"].n)
                e.op(V, TT(AB1[:, 2048 * tt + 512 * dc:
                               2048 * tt + 512 * (dc + 1)],
                           ZACC[:, 0:512], CROW[:, 0:512], ALU.add), CH["ve"])
                kv_evac_ve = CH["ve"].n
        dump("kpv", AB1[:, :])
        kvh_ve = {}
        kvh_st = {}
        for h in ([] if skip5 else range(NH)):
            e.wait(T, CH["ve"], kv_evac_ve)
            if h > 0:
                e.wait(T, CH["ve"], kvh_ve[h - 1])
            for tt in range(8):
                e.op(T, MM(PS_ACC[:, 0:128],
                           AB1[:, 2048 * tt + 128 * h:2048 * tt + 128 * (h + 1)],
                           AB1[:, 16384 + 2048 * tt + 128 * h:
                               16384 + 2048 * tt + 128 * (h + 1)],
                           tt == 0, tt == 7), CH["pe"])
            e.wait(V, CH["pe"], CH["pe"].n)
            sl_ = 128 * (h % 2)
            if h >= 2:
                e.wait(V, CH["st"], kvh_st[h - 2])
            e.op(V, lambda v, sl_=sl_: v.tensor_copy(
                MACC[:, sl_:sl_ + 128], PS_ACC[:, 0:128]), CH["ve"])
            kvh_ve[h] = CH["ve"].n
            e.wait(G, CH["ve"], kvh_ve[h])
            e.dma(G, ap["kv_in"][128 * h:128 * (h + 1), :],
                  MACC[:, sl_:sl_ + 128], CH["st"])
            kvh_st[h] = CH["st"].n
        for c4 in ([] if skip5 else range(4)):
            e.wait(T, CH["ve"], CH["ve"].n)
            for tt in range(8):
                e.op(T, MM(PS_ROW[:, 0:512], ONECOLB[:, 0:1],
                           AB1[:, 2048 * tt + 512 * c4:
                               2048 * tt + 512 * (c4 + 1)],
                           tt == 0, tt == 7), CH["pe"])
            e.wait(V, CH["pe"], CH["pe"].n)
            e.wait(V, CH["st"], CH["st"].n)
            vtiny(lambda v: v.tensor_copy(ZR[:, 0:512], PS_ROW[:, 0:512]))
            e.wait(G, CH["ve"], CH["ve"].n)
            for j in range(4):
                e.dma(G, ap["kv_in"][D + 4 * c4 + j:D + 4 * c4 + j + 1, :],
                      ZR[0:1, 128 * j:128 * (j + 1)], CH["st"])
        if "kvin" in debug:
            e.wait(G, CH["st"], CH["st"].n)
            e.dma(G, ap["dbg_kvin"][0:D, :], ap["kv_in"][0:D, :], CH["st"])
            e.dma(G, ap["dbg_kvin"][D:D + NH, :], ap["kv_in"][D:D + NH, :],
                  CH["st"])
        if stop_after == "kv":
            e.bar()
            e.wait(G, CH["st"], CH["st"].n)
            return
        e.wait(G, CH["st"], CH["st"].n)
        if no_coll:
            e.dma(G, ap["kv_out"][:, :], ap["kv_in"][:, :], CH["st"])
        else:
            e.op(G, lambda g: g.collective_compute(
                "AllReduce", ALU.add,
                replica_groups=[[0, 1], [2, 3], [4, 5], [6, 7]],
                ins=[apt["kv_in"].ap().opt()],
                outs=[apt["kv_out"].ap().opt()]), CH["cc"])
        if stop_after == "coll1":
            e.wait(G, CH["cc"], 0 if no_coll else 1)
            if "kvin" in debug:
                e.dma(G, ap["dbg_kvin"][0:D, :], ap["kv_out"][0:D, :],
                      CH["st"])
                e.dma(G, ap["dbg_kvin"][D:D + NH, :],
                      ap["kv_out"][D:D + NH, :], CH["st"])
            e.bar()
            e.wait(G, CH["st"], CH["st"].n)
            return

        def mx_quant_tile(src_bf, width, dst_bf, wait_pe, wait_st=0):
            nb = width // 32
            CR_I = CROW[:, :].bitcast(mybir.dt.int32)
            e.op(V, RED(CROW[:, 0:nb],
                        src_bf.rearrange("p (b k) -> p b k", k=32),
                        ALU.max, True), CH["ve"])
            VS()
            vtiny(TS(CR_I[:, nb:2 * nb], CROW[:, 0:nb].bitcast(mybir.dt.int32),
                     0x7F800000, None, ALU.bitwise_and))
            vtiny(TS(CR_I[:, 2 * nb:3 * nb], CR_I[:, nb:2 * nb],
                     float(7 << 23), -1.0, ALU.max, ALU.mult))
            vtiny(TS(CR_I[:, 2 * nb:3 * nb], CR_I[:, 2 * nb:3 * nb],
                     float(260 << 23), None, ALU.add))
            vtiny(TS(CR_I[:, 3 * nb:4 * nb], CR_I[:, 2 * nb:3 * nb],
                     -1.0, float(254 << 23), ALU.mult, ALU.add))
            e.op(V, lambda v: v.tensor_tensor(
                out=XQI[:, 0:width].rearrange("p (b k) -> p b k", k=32),
                in0=src_bf.rearrange("p (b k) -> p b k", k=32),
                in1=CROW[:, 2 * nb:3 * nb, None].broadcast_to([128, nb, 32]),
                op=ALU.mult), CH["ve"])
            e.wait(SE, CH["ve"], CH["ve"].n)
            e.wait(SE, CH["pe"], wait_pe)
            e.wait(SE, CH["st"], wait_st)
            e.op(SE, lambda s: s.copy(dst_bf, XQI[:, 0:width]), CH["se"])
            e.wait(V, CH["se"], CH["se"].n)
            e.op(V, lambda v: v.tensor_tensor(
                out=dst_bf.rearrange("p (b k) -> p b k", k=32),
                in0=dst_bf.rearrange("p (b k) -> p b k", k=32),
                in1=CROW[:, 3 * nb:4 * nb, None].broadcast_to([128, nb, 32]),
                op=ALU.mult), CH["ve"])
            return CH["ve"].n

        e.bar()
        wq_ve = {}
        wq_st = {}
        for j in ([] if skip55 else range(16)):
            buf = WQB[j % 2]
            bufF = buf[:, :].bitcast(F32)
            if j >= 2:
                e.wait(SY, CH["ve"], wq_ve[j - 2])
            if j < 8:
                src = ap["w1s"][128 * j:128 * (j + 1), :]
            else:
                r_, cc_ = (j - 8) // 4, (j - 8) % 4
                src = ap["w2s"][128 * r_:128 * (r_ + 1),
                                2048 * cc_:2048 * (cc_ + 1)]
            e.dma(SY, bufF, src, CH["cw"])
            e.wait(V, CH["cw"], CH["cw"].n)
            wq_ve[j] = mx_quant_tile(bufF, 2048, W1QB[:, :], 0,
                                     wait_st=(wq_st[j - 1] if j >= 1 else 0))
            e.wait(G, CH["ve"], wq_ve[j])
            if j < 8:
                dst = ap["ag1_in"][128 * j:128 * (j + 1), :]
            else:
                r_, cc_ = (j - 8) // 4, (j - 8) % 4
                dst = ap["ag2_in"][128 * r_:128 * (r_ + 1),
                                   2048 * cc_:2048 * (cc_ + 1)]
            e.dma(G, dst, W1QB[:, :], CH["st"])
            wq_st[j] = CH["st"].n
        e.wait(G, CH["st"], CH["st"].n)
        if no_coll:
            e.dma(G, ap["ag1_out"][0:1024, :], ap["ag1_in"][:, :], CH["st"])
            e.dma(G, ap["ag2_out"][0:256, :], ap["ag2_in"][:, :], CH["st"])
        else:
            e.op(G, lambda g: g.collective_compute(
                "AllGather", ALU.bypass,
                replica_groups=[[0, 1, 2, 3, 4, 5, 6, 7]],
                ins=[apt["ag1_in"].ap().opt()],
                outs=[apt["ag1_out"].ap().opt()]), CH["cc"])
            e.op(G, lambda g: g.collective_compute(
                "AllGather", ALU.bypass,
                replica_groups=[[0, 1, 2, 3, 4, 5, 6, 7]],
                ins=[apt["ag2_in"].ap().opt()],
                outs=[apt["ag2_out"].ap().opt()]), CH["cc"])
        if stop_after == "coll2":
            e.wait(G, CH["cc"], 0 if no_coll else 3)
            e.bar()
            e.wait(G, CH["st"], CH["st"].n)
            return

        e.bar()
        e.dma(G, WQB[0][:, 0:2048].rearrange("p (k c) -> p k c", k=16),
              ap["l1qwT"].rearrange("(k p) c -> p k c", p=128)[:, :, 0:128],
              CH["cwg"])
        qw_cw = {0: CH["cwg"].n}
        qp_ve = {}
        for i in range(qp_n):
            buf = WQB[i % 2]
            if i + 1 < qp_n:
                if i >= 1:
                    e.wait(G, CH["pe"], qp_pe[i - 1])
                e.dma(G, WQB[(i + 1) % 2][:, 0:2048].rearrange(
                    "p (k c) -> p k c", k=16),
                    ap["l1qwT"].rearrange("(k p) c -> p k c", p=128)[
                        :, :, 128 * (i + 1):128 * (i + 2)], CH["cwg"])
                qw_cw[i + 1] = CH["cwg"].n
            e.wait(T, CH["cwg"], qw_cw[i])
            if i > 0:
                e.wait(T, CH["ve"], qp_ve[i - 1])
            for c in range(2):
                for k in range(16):
                    e.op(T, MM(PS_ACC[:, 512 * c:512 * (c + 1)],
                               buf[:, 128 * k:128 * (k + 1)],
                               AB2[:, 1024 * k + 512 * c:
                                   1024 * k + 512 * (c + 1)],
                               k == 0, k == 15), CH["pe"])
            qp_pe = qp_pe if i > 0 else {}
            qp_pe[i] = CH["pe"].n
            e.wait(V, CH["pe"], qp_pe[i])
            if i > 0:
                e.wait(V, CH["se"], CH["se"].n)
            for c in (0, 512):
                e.op(V, TS(CROW[:, c:c + 512], PS_ACC[:, c:c + 512],
                           QBC[:, i:i + 1], None, ALU.add), CH["ve"])
            ve_z = CH["ve"].n
            e.wait(SE, CH["ve"], ve_z)
            e.op(SE, ACTF(ZACC[:, 0:1024], CROW[:, 0:1024], ACT.Relu),
                 CH["se"])
            e.op(V, TS(MACC[:, 0:1024], CROW[:, 0:1024], 0.0, None, ALU.min),
                 CH["ve"])
            ve_m = CH["ve"].n
            e.wait(SE, CH["ve"], ve_m)
            e.op(SE, ACTF(UB[0][:, :], MACC[:, 0:1024], ACT.Exp), CH["se"])
            e.wait(V, CH["se"], CH["se"].n)
            e.op(V, TT(AB1[:, 1024 * i:1024 * (i + 1)], ZACC[:, 0:1024],
                       UB[0][:, :], ALU.add), CH["ve"])
            qp_ve[i] = CH["ve"].n
        dump("qpT", AB1[:, 0:16384])
        if stop_after == "qp":
            e.bar()
            e.wait(G, CH["st"], CH["st"].n)
            return
        e.wait(G, CH["cc"], 0 if no_coll else 1)
        e.wait(G, CH["ve"], CH["ve"].n)
        e.dma(G, KTB[:].rearrange("p (k c) -> p k c", k=16),
              ap["kv_out"][0:D, :].rearrange("(k p) c -> p k c", p=128),
              CH["cwg"])
        e.dma(SY, CR8[:, :], ap["kv_out"][D:D + NH, :], CH["cx"])
        e.wait(T, CH["cx"], CH["cx"].n)
        e.wait(T, CH["ve"], CH["ve"].n)
        e.op(T, lambda t: t.transpose(PS_AUX[:, 0:16], CR8[:, :],
                                      IDENTF[0:16, 0:16]), CH["pe"])
        e.wait(V, CH["pe"], CH["pe"].n)
        KPCb = COLS[:, 56:64].bitcast(BF16)
        vtiny(lambda v: v.tensor_copy(KPCb[:, 0:16], PS_AUX[:, 0:16]))
        at_ve = {}
        for h in range(NH):
            e.wait(T, CH["cwg"], CH["cwg"].n)
            e.wait(T, CH["ve"], qp_ve[15])
            e.wait(T, CH["ve"], CH["ve"].n if h == 0 else at_ve[h - 1])
            for c in (0, 512):
                e.op(T, MM(PS_ACC[:, c:c + 512], KTB[:, 128 * h:128 * (h + 1)],
                           AB1[:, 1024 * h + c:1024 * h + c + 512],
                           True, True), CH["pe"])
            for c in (0, 512):
                e.wait(T, CH["ve"], CH["ve"].n)
                e.op(T, MM(PS_ROW[:, 0:512], KPCb[:, h:h + 1],
                           AB1[:, 1024 * h + c:1024 * h + c + 512],
                           True, True), CH["pe"])
                e.wait(V, CH["pe"], CH["pe"].n)
                vtiny(TS(ZR[:, 0:512], PS_ROW[:, 0:512], 1e-6, None, ALU.add))
                vtiny(lambda v: v.reciprocal(BROWF[:, 0:512], ZR[:, 0:512]))
                e.wait(T, CH["ve"], CH["ve"].n)
                e.op(T, MM(PS_AUX[:, 0:512], ONE1F[:], BROWF[:, 0:512],
                           True, True), CH["pe"])
                e.wait(V, CH["pe"], CH["pe"].n)
                e.op(V, lambda v, c=c: v.tensor_copy(MACC[:, c:c + 512],
                                                     PS_AUX[:, 0:512]),
                     CH["ve"])
                e.op(V, TT(AB2[:, 1024 * h + c:1024 * h + c + 512],
                           PS_ACC[:, c:c + 512], MACC[:, c:c + 512],
                           ALU.mult), CH["ve"])
            at_ve[h] = CH["ve"].n
        dump("attnT", AB2[:, :])
        if stop_after == "attn1":
            e.bar()
            e.wait(G, CH["st"], CH["st"].n)
            return
        op_pe = {}
        op_st = {}
        for dc in range(4):
            bo = 16384 + 8192 * (dc % 2)
            if dc >= 2:
                e.wait(G, CH["pe"], op_pe[dc - 2])
            e.dma(G, AB1[:, bo:bo + 8192].rearrange("p (k c) -> p k c", k=16),
                  ap["l1owT"].rearrange("(k p) c -> p k c", p=128)[
                      :, :, 512 * dc:512 * (dc + 1)], CH["cwg"])
            cw_o = CH["cwg"].n
            e.wait(SY, CH["pe"], CH["pe"].n)
            e.wait(SY, CH["ve"], CH["ve"].n)
            e.dma(SY, BROWF[:], ap["l1ob"][0:1, 512 * dc:512 * (dc + 1)],
                  CH["cx"])
            e.wait(T, CH["cx"], CH["cx"].n)
            e.wait(T, CH["ve"], CH["ve"].n)
            e.op(T, MM(PS_AUX[:, 0:512], ONE1F[:], BROWF[:], True, True),
                 CH["pe"])
            e.wait(V, CH["pe"], CH["pe"].n)
            e.op(V, lambda v: v.tensor_copy(UB[0][:, 0:512], PS_AUX[:, 0:512]),
                 CH["ve"])
            ob_ve = CH["ve"].n
            for tt in range(8):
                pss = (PS_A, PS_B)[tt % 2]
                e.wait(T, CH["cwg"], cw_o)
                e.wait(T, CH["ve"], at_ve[NH - 1])
                e.wait(T, CH["ve"], ob_ve)
                if (dc, tt) > (0, 1):
                    e.wait(T, CH["ve"], CH["ve"].n)
                for k in range(16):
                    e.op(T, MM(pss[:], AB2[:, 1024 * k + 128 * tt:
                                           1024 * k + 128 * (tt + 1)],
                               AB1[:, bo + 512 * k:bo + 512 * (k + 1)],
                               k == 0, k == 15), CH["pe"])
                op_pe[dc] = CH["pe"].n
                e.wait(V, CH["pe"], CH["pe"].n)
                sl_ = 512 * (tt % 2)
                if (dc, tt) > (0, 1):
                    e.wait(V, CH["st"], op_st[(dc * 8 + tt) - 2])
                e.op(V, TT(EXPB[:, sl_:sl_ + 512], pss[:], UB[0][:, 0:512],
                           ALU.add), CH["ve"])
                e.wait(G, CH["ve"], CH["ve"].n)
                e.dma(G, ap["s1_b"][128 * tt:128 * (tt + 1),
                                    512 * dc:512 * (dc + 1)],
                      EXPB[:, sl_:sl_ + 512], CH["st"])
                op_st[dc * 8 + tt] = CH["st"].n
        if "s1b" in debug:
            e.wait(G, CH["st"], CH["st"].n)
            e.dma(G, ap["dbg_s1b"][:, :], ap["s1_b"][:, :], CH["st"])
        if stop_after == "s1":
            e.bar()
            e.wait(G, CH["st"], CH["st"].n)
            return

        KTBf = KTB[:, :].bitcast(F32)
        VHBf = VHB[:, :].bitcast(F32)
        W1QBf = W1QB[:, :].bitcast(F32)
        XQIb = XQI[:, :].bitcast(BF16)
        mm2g = [PS_A[:, :], PS_B[:, :], PS_ACC[:, 0:512], PS_ACC[:, 512:1024]]
        for half in range(2):
            e.bar()
            if half == 0:
                e.wait(SY, CH["cc"], 0 if no_coll else 3)
            for ci, dst in ((0, KTBf[:, 0:512]), (1, KTBf[:, 512:1024]),
                            (2, VHBf[:, 0:512]), (3, VHBf[:, 512:1024]),
                            (4, W1QBf[:, 0:512]), (5, W1QBf[:, 512:1024]),
                            (6, UB[0][:, 0:512]), (7, UB[0][:, 512:1024])):
                nm = "l1lnw" if ci < 4 else "l1lnb"
                co = 512 * (ci % 4)
                e.wait(SY, CH["pe"], CH["pe"].n)
                e.dma(SY, BROWF[:], ap[nm][0:1, co:co + 512], CH["cx"])
                e.wait(T, CH["cx"], CH["cx"].n)
                e.wait(T, CH["ve"], CH["ve"].n)
                e.op(T, MM(PS_AUX[:, 0:512], ONE1F[:], BROWF[:], True, True),
                     CH["pe"])
                e.wait(V, CH["pe"], CH["pe"].n)
                e.op(V, lambda v, dst=dst: v.tensor_copy(dst, PS_AUX[:, 0:512]),
                     CH["ve"])
            rep_ve = CH["ve"].n
            xm_pe = {}
            for tt in range(4):
                t = 4 * half + tt
                if tt > 0:
                    e.wait(SY, CH["ve"], CH["ve"].n)
                e.dma(SY, WQB[0][:, :].bitcast(F32),
                      ap["x1_b"][128 * t:128 * (t + 1), :], CH["cx"])
                e.wait(V, CH["cx"], CH["cx"].n)
                W0f = WQB[0][:, :].bitcast(F32)
                e.op(V, TS(MACC[:], W0f[:, 0:1024], COLS[:, t:t + 1],
                           COLS[:, 8 + t:9 + t], ALU.add, ALU.mult), CH["ve"])
                e.op(V, TS(ZACC[:], W0f[:, 1024:2048], COLS[:, t:t + 1],
                           COLS[:, 8 + t:9 + t], ALU.add, ALU.mult), CH["ve"])
                xbf_t = XBFv[:, 2048 * tt:2048 * (tt + 1)]
                e.op(V, lambda v, xbf_t=xbf_t: v.tensor_copy(
                    xbf_t[:, 0:1024], MACC[:, :]), CH["ve"])
                e.op(V, lambda v, xbf_t=xbf_t: v.tensor_copy(
                    xbf_t[:, 1024:2048], ZACC[:, :]), CH["ve"])
                ln_stats(xbf_t, 36, 37)
                e.wait(V, CH["ve"], rep_ve)
                e.op(V, TS(MACC[:], xbf_t[:, 0:1024], COLS[:, 36:37],
                           COLS[:, 37:38], ALU.add, ALU.mult), CH["ve"])
                e.op(V, TS(ZACC[:], xbf_t[:, 1024:2048], COLS[:, 36:37],
                           COLS[:, 37:38], ALU.add, ALU.mult), CH["ve"])
                e.op(V, TT(MACC[:], MACC[:], KTBf[:, :], ALU.mult), CH["ve"])
                e.op(V, TT(ZACC[:], ZACC[:], VHBf[:, :], ALU.mult), CH["ve"])
                e.op(V, TT(WQB[1][:, 0:1024], MACC[:], W1QBf[:, :], ALU.add),
                     CH["ve"])
                e.op(V, TT(WQB[1][:, 1024:2048], ZACC[:], UB[0][:, :], ALU.add),
                     CH["ve"])
                if tt == 0:
                    wait_pe = 0 if half == 0 else prev_xm_pe
                else:
                    wait_pe = xm_pe[tt - 1]
                q_ve = mx_quant_tile(WQB[1][:, 0:2048], 2048,
                                     WQB[1][:, 2048:4096], wait_pe)
                xm_pe[tt] = transpose16(WQB[1], AB2, 512, 128 * tt, q_ve,
                                        src_off=2048)
            prev_xm_pe = xm_pe[3]
            if half == 0:
                dump("xmxT", AB2[:, :])
            e.bar()
            mm1_ev = 0
            mm1_g = {}
            w1b_pe = {}
            for fc in range(16):
                wbo = 8192 * (fc % 2)
                if fc >= 2:
                    e.wait(SY, CH["pe"], w1b_pe[fc - 2])
                for dt in range(16):
                    e.dma(SY, AB3[:, wbo + 512 * dt:wbo + 512 * (dt + 1)],
                          ap["ag1_out"][512 * fc:512 * (fc + 1),
                                        128 * dt:128 * (dt + 1)],
                          CH["cw"], transpose=True)
                cw_m = CH["cw"].n
                e.wait(SY, CH["pe"], CH["pe"].n)
                e.wait(SY, CH["ve"], CH["ve"].n)
                e.dma(SY, BROWF[:], ap["l1b1"][0:1, 512 * fc:512 * (fc + 1)],
                      CH["cx"])
                e.wait(T, CH["cx"], CH["cx"].n)
                e.wait(T, CH["ve"], CH["ve"].n)
                e.op(T, MM(PS_AUX[:, 0:512], ONE1F[:], BROWF[:], True, True),
                     CH["pe"])
                e.wait(V, CH["pe"], CH["pe"].n)
                e.op(V, lambda v: v.tensor_copy(CROW[:, 512:1024],
                                                PS_AUX[:, 0:512]), CH["ve"])
                b1_ve = CH["ve"].n
                for tt in range(4):
                    e.wait(T, CH["cw"], cw_m)
                    e.wait(T, CH["ve"], mm1_g.get(tt, 0))
                    e.wait(T, CH["ve"], b1_ve)
                    for dt in range(16):
                        e.op(T, MM(mm2g[tt],
                                   AB2[:, 512 * dt + 128 * tt:
                                       512 * dt + 128 * (tt + 1)],
                                   AB3[:, wbo + 512 * dt:wbo + 512 * (dt + 1)],
                                   dt == 0, dt == 15), CH["pe"])
                    w1b_pe[fc] = CH["pe"].n
                    e.wait(V, CH["pe"], CH["pe"].n)
                    e.wait(V, CH["se"], CH["se"].n)
                    e.op(V, TT(EXPB[:, 0:512], mm2g[tt], CROW[:, 512:1024],
                               ALU.add), CH["ve"])
                    mm1_g[tt] = CH["ve"].n
                    e.wait(SE, CH["ve"], mm1_g[tt])
                    e.op(SE, ACTF(MACC[:, 0:512], EXPB[:, 0:512],
                                  ACT.Gelu_apprx_tanh), CH["se"])
                    e.op(SE, lambda s: s.copy(EXPB[:, 512:1024],
                                              MACC[:, 0:512]), CH["se"])
                    gel_se = CH["se"].n
                    e.wait(V, CH["se"], gel_se)
                    q2_ve = mx_quant_tile(EXPB[:, 512:1024], 512,
                                          XQIb[:, 512:1024], CH["pe"].n)
                    e.wait(T, CH["ve"], q2_ve)
                    e.wait(T, CH["se"], CH["se"].n)
                    for jj in range(4):
                        e.op(T, lambda t, jj=jj: t.transpose(
                            PS_Sb[:, 128 * jj:128 * (jj + 1)],
                            XQIb[:, 512 + 128 * jj:640 + 128 * jj],
                            IDENT[:]), CH["pe"])
                    e.wait(SE, CH["pe"], CH["pe"].n)
                    e.op(SE, lambda s, fc=fc, tt=tt: s.copy(
                        AB1[:, :].rearrange("p (f c) -> p f c", c=512)[
                            :, 4 * fc:4 * fc + 4, 128 * tt:128 * (tt + 1)],
                        PS_Sb[:, 0:512].rearrange("p (j c) -> p j c", j=4)),
                        CH["se"])
            if half == 0:
                dump("xmx2T", AB1[:, :])
            e.bar()
            gate_ve = 0
            gate_st = 0
            mm2_pe = {}
            for dc in range(4):
                e.wait(SY, CH["pe"], CH["pe"].n)
                e.wait(SY, CH["ve"], CH["ve"].n)
                e.dma(SY, BROWF[:], ap["l1b2"][0:1, 512 * dc:512 * (dc + 1)],
                      CH["cx"])
                e.wait(T, CH["cx"], CH["cx"].n)
                e.wait(T, CH["ve"], CH["ve"].n)
                e.op(T, MM(PS_AUX[:, 0:512], ONE1F[:], BROWF[:], True, True),
                     CH["pe"])
                e.wait(V, CH["pe"], CH["pe"].n)
                e.op(V, lambda v: v.tensor_copy(CROW[:, 512:1024],
                                                PS_AUX[:, 0:512]), CH["ve"])
                b2_ve = CH["ve"].n
                for fb in range(16):
                    bo2 = 8192 + 2048 * (fb % 2)
                    if fb >= 2:
                        e.wait(SY, CH["pe"], mm2_pe[fb - 2])
                    elif dc > 0:
                        e.wait(SY, CH["pe"], mm2_pe[14 + fb])
                    for ff in range(4):
                        ft = 4 * fb + ff
                        e.dma(SY, AB2[:, bo2 + 512 * ff:bo2 + 512 * (ff + 1)],
                              ap["ag2_out"][512 * dc:512 * (dc + 1),
                                            128 * ft:128 * (ft + 1)],
                              CH["cw"], transpose=True)
                    e.wait(T, CH["cw"], CH["cw"].n)
                    if fb == 0:
                        e.wait(T, CH["ve"], b2_ve)
                        if dc > 0:
                            e.wait(T, CH["ve"], gate_ve)
                    for ff in range(4):
                        ft = 4 * fb + ff
                        for ts in range(4):
                            e.op(T, MM(mm2g[ts],
                                       AB1[:, 512 * ft + 128 * ts:
                                           512 * ft + 128 * (ts + 1)],
                                       AB2[:, bo2 + 512 * ff:
                                           bo2 + 512 * (ff + 1)],
                                       ft == 0, ft == 63), CH["pe"])
                    mm2_pe = mm2_pe if (fb > 0 or dc > 0) else {}
                    mm2_pe[fb] = CH["pe"].n
                pe_m2 = CH["pe"].n
                for ts in range(4):
                    t = 4 * half + ts
                    e.wait(V, CH["pe"], pe_m2)
                    e.wait(V, CH["se"], CH["se"].n)
                    e.op(V, TT(EXPB[:, 0:512], mm2g[ts], CROW[:, 512:1024],
                               ALU.add), CH["ve"])
                    e.op(V, TT(EXPB[:, 512:1024],
                               XBFv[:, 2048 * ts + 512 * dc:
                                    2048 * ts + 512 * (dc + 1)],
                               EXPB[:, 0:512], ALU.add), CH["ve"])
                    ve_of = CH["ve"].n
                    e.wait(SE, CH["ve"], ve_of)
                    if gate_ve > 0:
                        e.wait(SE, CH["ve"], gate_ve)
                    e.op(SE, ACTF(MACC[:, 0:512], EXPB[:, 512:1024],
                                  ACT.Silu), CH["se"])
                    se_g1 = CH["se"].n
                    if gate_ve > 0:
                        e.wait(SY, CH["ve"], gate_ve)
                    e.dma(SY, ZACC[:, 0:512],
                          ap["x1_b"][128 * t:128 * (t + 1),
                                     512 * dc:512 * (dc + 1)], CH["cx"])
                    e.dma(SY, XQIb[:, 0:512],
                          ap["s1_b"][128 * t:128 * (t + 1),
                                     512 * dc:512 * (dc + 1)], CH["cx"])
                    cx_g = CH["cx"].n
                    e.wait(V, CH["se"], se_g1)
                    e.wait(V, CH["cx"], cx_g)
                    if gate_st > 0:
                        e.wait(V, CH["st"], gate_st)
                    e.op(V, TT(MACC[:, 512:1024], XQIb[:, 0:512],
                               MACC[:, 0:512], ALU.mult), CH["ve"])
                    e.op(V, TT(ZACC[:, 512:1024], ZACC[:, 0:512],
                               MACC[:, 512:1024], ALU.add), CH["ve"])
                    gate_ve = CH["ve"].n
                    e.wait(G, CH["ve"], gate_ve)
                    e.dma(G, ap["out"][128 * t:128 * (t + 1),
                                       512 * dc:512 * (dc + 1)],
                          ZACC[:, 512:1024], CH["st"])
                    gate_st = CH["st"].n
        e.bar()
        e.wait(G, CH["st"], CH["st"].n)

    with nc.Block() as block:
        for nm in CHN:
            if nm in ("cx", "cw", "cwg", "st"):
                chans[nm] = DmaGrp(nc, nm, nsub=4)
            else:
                chans[nm] = Ch(nc, nm)

        def run(live, eng):
            for c in chans.values():
                if isinstance(c, DmaGrp):
                    c.reset()
                else:
                    c.n = 0
            program(Emit(nc, live, eng, chans))

        @block.sync
        def _(eng):
            run("sync", eng)

        @block.tensor
        def _(eng):
            run("tensor", eng)

        @block.vector
        def _(eng):
            run("vector", eng)

        @block.scalar
        def _(eng):
            run("scalar", eng)

        @block.gpsimd
        def _(eng):
            run("gpsimd", eng)

    for cm in reversed(ctxs):
        cm.__exit__(None, None, None)
    return nc, ap


def rope_tables():
    inv_freq = 1.0 / 10000.0 ** (np.arange(0, HD, 2, dtype=np.float32) / HD)
    t_ = np.arange(S, dtype=np.float32)
    freqs = np.einsum("i,j->ij", t_, inv_freq)
    emb = np.concatenate((freqs, freqs), -1)
    return np.cos(emb), np.sin(emb)


_COS, _SIN = rope_tables()


def perm_for(p):
    blocks = [2 * j + p for j in range(8)] + [2 * j + 1 - p for j in range(8)]
    idx = np.concatenate([np.arange(128) + 128 * g for g in blocks])
    return np.asarray(blocks), idx


def prep_core_inputs(I, c):
    b, p = c >> 1, c & 1
    blocks, idx = perm_for(p)
    x0 = np.ascontiguousarray(np.asarray(I["L0_attn_hidden_states"][b],
                                         dtype=np.float32)[idx])
    cosP = _COS[idx]
    sinP = _SIN[idx]
    cosT = np.ascontiguousarray(cosP.T)
    sinTn = sinP.T.copy()
    sinTn[0:64] = -sinTn[0:64]
    sinTn = np.ascontiguousarray(sinTn)
    qq = np.arange(128)
    m0 = (qq[:, None] <= qq[None, :]).astype(BF)
    m1 = (np.ones((128, 128)) * (1.0 if p == 1 else 0.0)).astype(BF)
    cmask = np.stack([m0, m1])
    T_ = lambda a: np.ascontiguousarray(np.asarray(a, np.float32).T)
    C_ = lambda a: np.ascontiguousarray(np.asarray(a, np.float32))
    d = {
        "x0": x0,
        "wqT": T_(I["L0_attn_Wq"]), "wkT": T_(I["L0_attn_Wk"]),
        "wvT": T_(I["L0_attn_Wv"]), "woT": T_(I["L0_attn_Wo"]),
        "w1T": T_(I["L0_ffn_w1"]), "w2T": T_(I["L0_ffn_w2"]),
        "l1qwT": T_(I["L1_attn_q_w"]), "l1kwT": T_(I["L1_attn_k_w"]),
        "l1vwT": T_(I["L1_attn_v_w"]), "l1owT": T_(I["L1_attn_o_w"]),
        "l1qb": np.ascontiguousarray(
            np.asarray(I["L1_attn_q_b"], np.float32).reshape(16, 128).T),
        "l1kb": C_(I["L1_attn_k_b"]).reshape(1, D),
        "l1vb": C_(I["L1_attn_v_b"]).reshape(1, D),
        "l1ob": C_(I["L1_attn_o_b"]).reshape(1, D),
        "l1lnw": C_(I["L1_ffn_ln_w"]).reshape(1, D),
        "l1lnb": C_(I["L1_ffn_ln_b"]).reshape(1, D),
        "w1s": C_(I["L1_ffn_W1"][1024 * c:1024 * (c + 1)]),
        "l1b1": C_(I["L1_ffn_b1"]).reshape(1, FF),
        "w2s": C_(I["L1_ffn_W2"][256 * c:256 * (c + 1)]),
        "l1b2": C_(I["L1_ffn_b2"]).reshape(1, D),
        "cosT": cosT, "sinTn": sinTn,
        "ident": np.eye(128, dtype=BF),
        "identf": np.eye(128, dtype=np.float32),
        "cmask": cmask,
    }
    return d


def assemble(results, out_dtype=np.float32):
    out = np.zeros((4, S, D), dtype=out_dtype)
    for c in range(8):
        b, p = c >> 1, c & 1
        _, idx = perm_for(p)
        out[b, idx[:SOWN]] = results[c]["out"]
    return out


def kernel(**inputs):
    nc, _ap = build()
    in_maps = [prep_core_inputs(inputs, c) for c in range(8)]
    res = run_bass_kernel_spmd(nc, in_maps, core_ids=list(range(8)))
    return assemble([res.results[c] for c in range(8)])


# version 02 (final)
# speedup vs baseline: 1.0170x; 1.0170x over previous
import numpy as np
import ml_dtypes
import concourse.bass as bass
import concourse.mybir as mybir
from concourse.bass_utils import run_bass_kernel_spmd

F32 = mybir.dt.float32
BF16 = mybir.dt.bfloat16
I8 = mybir.dt.int8
ALU = mybir.AluOpType
ACT = mybir.ActivationFunctionType
AXX = mybir.AxisListType.X

D, FF, NH, HD, S, SOWN = 2048, 8192, 16, 128, 2048, 1024
NT, NTF = 8, 16
ISQ = 0.08838834764831845
BF = ml_dtypes.bfloat16


class Ch:
    def __init__(self, nc, name):
        self.cm = nc.semaphore(name)
        self.sem = self.cm.__enter__()
        self.n = 0


class DmaGrp:

    def __init__(self, nc, name, nsub=4):
        self.subs = [Ch(nc, f"{name}{i}") for i in range(nsub)]
        self.hist = []
        self.n = 0

    def reset(self):
        self.n = 0
        self.hist = []
        for s in self.subs:
            s.n = 0

    def watermark(self, val):
        k = min(val // 16, len(self.hist))
        if k <= 0:
            return [0] * len(self.subs)
        return self.hist[k - 1]


class Emit:
    def __init__(self, nc, live, eng, chans):
        self.nc, self.live, self.eng, self.ch = nc, live, eng, chans

    def wait(self, engine, ch, val):
        if isinstance(ch, DmaGrp):
            if self.live == engine:
                for s, c in zip(ch.subs, ch.watermark(val)):
                    if c > 0:
                        self.eng.wait_ge(s.sem, c)
            return
        if self.live == engine:
            self.eng.wait_ge(ch.sem, val)

    def inc(self, instr, ch, n):
        if instr is not None:
            instr.then_inc(ch.sem, n)
        ch.n += n

    def dma(self, engine, out, in_, ch, **kw):
        if isinstance(ch, DmaGrp):
            sub = ch.subs[(ch.n // 16) % len(ch.subs)]
            if self.live == engine:
                if sub.n > 0:
                    self.eng.wait_ge(sub.sem, sub.n)
                self.eng.dma_start(out=out, in_=in_, **kw).then_inc(sub.sem, 16)
            sub.n += 16
            ch.n += 16
            ch.hist.append(tuple(s.n for s in ch.subs))
            return
        i = self.eng.dma_start(out=out, in_=in_, **kw) if self.live == engine else None
        self.inc(i, ch, 16)

    def op(self, engine, fn, ch=None):
        i = fn(self.eng) if self.live == engine else None
        if ch is not None:
            self.inc(i, ch, 1)
        return i

    def bar(self):
        for nm in ("pe", "ve", "se", "ge", "st", "cw", "cwg", "cx", "cc"):
            ch = self.ch.get(nm)
            if ch is None:
                continue
            if isinstance(ch, DmaGrp):
                for s in ch.subs:
                    if s.n > 0:
                        self.eng.wait_ge(s.sem, s.n)
            elif ch.n > 0:
                self.eng.wait_ge(ch.sem, ch.n)


def build(stop_after=None, debug=(), paranoid=False, no_coll=False, qp_n=16, skip5=False, skip55=False):
    nc = bass.Bass("TRN2", target_bir_lowering=False, detect_race_conditions=False)
    ap = {}

    def inp(name, shape, dt=F32):
        ap[name] = nc.dram_tensor(name, list(shape), dt, kind="ExternalInput").ap()

    def outp(name, shape, dt=F32):
        ap[name] = nc.dram_tensor(name, list(shape), dt, kind="ExternalOutput").ap()

    def dram(name, shape, dt):
        ap[name] = nc.dram_tensor(name, list(shape), dt).ap()

    inp("x0", (S, D))
    inp("wqT", (D, D)); inp("wkT", (D, D)); inp("wvT", (D, D)); inp("woT", (D, D))
    inp("w1T", (D, FF)); inp("w2T", (FF, D))
    inp("l1qwT", (D, D)); inp("l1kwT", (D, D)); inp("l1vwT", (D, D)); inp("l1owT", (D, D))
    inp("l1qb", (128, 16)); inp("l1kb", (1, D)); inp("l1vb", (1, D)); inp("l1ob", (1, D))
    inp("l1lnw", (1, D)); inp("l1lnb", (1, D))
    inp("w1s", (FF // 8, D)); inp("l1b1", (1, FF))
    inp("w2s", (D // 8, FF)); inp("l1b2", (1, D))
    inp("cosT", (HD, S)); inp("sinTn", (HD, S))
    DBG = {"xnT": ((128, 32768), BF16), "s0": ((128, 16384), BF16),
           "aoT": ((128, 16384), BF16), "xqT": ((128, 16384), BF16),
           "cols": ((128, 64), F32), "crow": ((128, 1024), F32),
           "ub": ((FF, SOWN), F32), "n2": ((128, 32768), BF16),
           "qT": ((128, 1024), BF16), "kT": ((128, 2048), BF16),
           "wost": ((128, 2048), BF16),
           "v4": ((128, 16384), BF16), "zacc": ((128, 1024), F32),
           "xn1T": ((128, 16384), BF16), "kpv": ((128, 32768), BF16),
           "kvin": ((D + NH, HD), F32), "qpT": ((128, 16384), BF16),
           "attnT": ((128, 16384), BF16), "s1b": ((SOWN, D), BF16),
           "xmxT": ((128, 16384), BF16), "xmx2T": ((128, 32768), BF16)}
    for nm in debug:
        outp("dbg_" + nm, DBG[nm][0], DBG[nm][1])
    inp("ident", (128, 128), BF16)
    inp("identf", (128, 128), F32)
    inp("cmask", (2, 128, 128), BF16)
    outp("out", (SOWN, D))

    apt = {}

    def dramt(name, shape, dt, **kw):
        apt[name] = nc.dram_tensor(name, list(shape), dt, **kw)
        ap[name] = apt[name].ap()

    dram("u_b", (FF, SOWN), F32)
    dram("x1_b", (SOWN, D), F32)
    dram("w2q_b", (FF, D), BF16)
    dram("s1_b", (SOWN, D), BF16)
    dramt("kv_in", (D + NH, HD), F32)
    dramt("kv_out", (D + NH, HD), F32)
    dramt("ag1_in", (FF // 8, D), BF16)
    dramt("ag1_out", (FF, D), BF16, addr_space="Shared")
    dramt("ag2_in", (D // 8, FF), BF16)
    dramt("ag2_out", (D, FF), BF16, addr_space="Shared")

    ctxs = []

    def sbuf(name, shape, dt):
        cm = nc.sbuf_tensor(name, list(shape), dt)
        t = cm.__enter__(); ctxs.append(cm)
        return t

    def psum(name, shape, dt=F32):
        cm = nc.psum_tensor(name, list(shape), dt)
        t = cm.__enter__(); ctxs.append(cm)
        return t

    AB1 = sbuf("AB1", (128, 32768), BF16)
    AB2 = sbuf("AB2", (128, 16384), BF16)
    AB3 = sbuf("AB3", (128, 16384), BF16)
    AF2 = sbuf("AF2", (128, 5120), F32)
    WQB = [sbuf("WQB0", (128, 4096), BF16), sbuf("WQB1", (128, 4096), BF16)]
    KTB = sbuf("KTB", (128, S), BF16)
    VHB = sbuf("VHB", (128, S), BF16)
    EXPB = sbuf("EXPB", (128, 1024), BF16)
    ZACC = sbuf("ZACC", (128, SOWN), F32)
    MACC = sbuf("MACC", (128, SOWN), F32)
    CROW = sbuf("CROW", (128, 1024), F32)
    UB = [sbuf("UB0", (128, 1024), F32), sbuf("UB1", (128, 1024), F32)]
    XQI = sbuf("XQI", (128, 2048), I8)
    W1QB = sbuf("W1QB", (128, 2048), BF16)
    COLS = sbuf("COLS", (128, 64), F32)
    QBC = sbuf("QBC", (128, 16), F32)
    IDENT = sbuf("IDENT", (128, 128), BF16)
    IDENTF = sbuf("IDENTF", (128, 128), F32)
    CMASK = sbuf("CMASK", (128, 256), BF16)
    ONE1F = sbuf("ONE1F", (1, 128), F32)
    ONESB = sbuf("ONESB", (128, 128), BF16)
    ONECOL = sbuf("ONECOL", (128, 4), F32)
    ONECOLB = sbuf("ONECOLB", (128, 4), BF16)
    BROWF = sbuf("BROWF", (1, 512), F32)
    ZR = sbuf("ZR", (1, 512), F32)
    CR8 = sbuf("CR8", (16, 128), F32)

    PS_S = psum("PS_S", (128, 1024), F32)
    PS_ACC = psum("PS_ACC", (128, 1024), F32)
    PS_A = psum("PS_A", (128, 512), F32)
    PS_B = psum("PS_B", (128, 512), F32)
    PS_AUX = psum("PS_AUX", (128, 512), F32)
    PS_ROW = psum("PS_ROW", (1, 512), F32)
    PS_Sb = PS_S[:, :].bitcast(BF16)
    W1QB2 = CROW[:, :].bitcast(BF16)
    BAR2 = W1QB[:, 0:512]
    KP16 = CR8
    WQI = XQI
    QTB = VHB[:, 0:1024]

    xnT = AB1
    aoT = AB2
    xqT = AB2
    s0 = AB3
    cosT_s = AF2[:, 0:2048]
    sinTn_s = AF2[:, 2048:4096]
    xst = [AF2[:, 0:2048], AF2[:, 2048:4096]]
    sq2 = ZACC[:, 0:1024]
    WSTv = [AF2[:, 0:2048], AF2[:, 2048:4096]]
    XBFv = AF2[:, 0:4096].bitcast(BF16)

    chans = {}
    CHN = ["bar", "cx", "cw", "cwg", "pe", "ve", "se", "ge", "st", "cc"]

    def program(e: Emit):
        CH = e.ch
        V, SE, T, G, SY = "vector", "scalar", "tensor", "gpsimd", "sync"

        def dump(nm, src_ap):
            if nm in debug:
                e.wait(G, CH["ve"], CH["ve"].n)
                e.wait(G, CH["se"], CH["se"].n)
                n = src_ap.shape[1]
                step = 8192
                for o in range(0, n, step):
                    w = min(step, n - o)
                    e.dma(G, ap["dbg_" + nm][:, o:o + w], src_ap[:, o:o + w],
                          CH["st"])

        def TS(out, in0, s1, s2, op0, op1=None):
            return lambda v: v.tensor_scalar(
                out=out, in0=in0, scalar1=s1, scalar2=s2, op0=op0,
                op1=(op1 if op1 is not None else ALU.bypass))

        def TT(out, in0, in1, op):
            return lambda v: v.tensor_tensor(out=out, in0=in0, in1=in1, op=op)

        def MM(out, lhsT, rhs, start, stop):
            return lambda t: t.matmul(out, lhsT, rhs, start=start, stop=stop)

        def RED(out, in_, op, absval=False):
            return lambda v: v.tensor_reduce(out=out, in_=in_, axis=AXX, op=op,
                                             apply_absolute_value=absval)

        def VS():
            e.wait(V, CH["ve"], CH["ve"].n)

        def vtiny(fn):
            e.op(V, fn, CH["ve"])
            VS()

        def ACTF(out, in_, func, bias=None, scale=None):
            def f(s):
                kw = {}
                if bias is not None:
                    kw["bias"] = bias
                if scale is not None:
                    kw["scale"] = scale
                return s.activation(out=out, in_=in_, func=func, **kw)
            return f

        e.dma(SY, IDENT[:], ap["ident"], CH["cx"])
        e.dma(SY, IDENTF[:], ap["identf"], CH["cx"])
        e.dma(SY, CMASK[:, 0:128], ap["cmask"][0], CH["cx"])
        e.dma(SY, CMASK[:, 128:256], ap["cmask"][1], CH["cx"])
        e.dma(SY, QBC[:], ap["l1qb"], CH["cx"])
        for g_ in [lambda g: g.memset(COLS[:], 0.0),
                   lambda g: g.memset(ONE1F[:], 1.0),
                   lambda g: g.memset(ONESB[:], 1.0),
                   lambda g: g.memset(ONECOL[:], 1.0),
                   lambda g: g.memset(ONECOLB[:], 1.0),
                   lambda g: g.memset(BAR2[:], 0.0),
                   ]:
            e.op(G, g_, CH["ge"])
        e.bar()

        def ln_stats(xt_, cneg, cr):
            c0, c1, c2 = 32, 33, 34
            e.op(V, RED(COLS[:, c0:c0 + 1], xt_[:], ALU.add), CH["ve"])
            VS()
            vtiny(TS(COLS[:, cneg:cneg + 1], COLS[:, c0:c0 + 1], -1.0 / D, None,
                     ALU.mult))
            e.wait(SE, CH["ve"], CH["ve"].n)
            e.op(SE, ACTF(MACC[:], xt_[:, 0:1024], ACT.Square,
                          bias=COLS[:, cneg:cneg + 1]), CH["se"])
            e.op(SE, ACTF(sq2[:], xt_[:, 1024:2048], ACT.Square,
                          bias=COLS[:, cneg:cneg + 1]), CH["se"])
            e.wait(V, CH["se"], CH["se"].n)
            e.op(V, RED(COLS[:, c0:c0 + 1], MACC[:], ALU.add), CH["ve"])
            e.op(V, RED(COLS[:, c1:c1 + 1], sq2[:], ALU.add), CH["ve"])
            VS()
            vtiny(TT(COLS[:, c0:c0 + 1], COLS[:, c0:c0 + 1], COLS[:, c1:c1 + 1],
                     ALU.add))
            vtiny(TS(COLS[:, c0:c0 + 1], COLS[:, c0:c0 + 1], 1.0 / D, 1e-5,
                     ALU.mult, ALU.add))
            e.wait(SE, CH["ve"], CH["ve"].n)
            e.op(SE, ACTF(COLS[:, c1:c1 + 1], COLS[:, c0:c0 + 1], ACT.Sqrt), CH["se"])
            e.wait(V, CH["se"], CH["se"].n)
            if paranoid and cneg < 8:
                e.op(V, lambda v: v.tensor_copy(COLS[:, 52 + cneg:53 + cneg],
                                                COLS[:, c0:c0 + 1]), CH["ve"])
                e.op(V, lambda v: v.tensor_copy(COLS[:, 44 + cneg:45 + cneg],
                                                COLS[:, c1:c1 + 1]), CH["ve"])
            vtiny(lambda v: v.reciprocal(COLS[:, cr:cr + 1], COLS[:, c1:c1 + 1]))
            vtiny(TT(COLS[:, c2:c2 + 1], COLS[:, cr:cr + 1], COLS[:, cr:cr + 1],
                     ALU.mult))
            vtiny(TT(COLS[:, c2:c2 + 1], COLS[:, c2:c2 + 1], COLS[:, c0:c0 + 1],
                     ALU.mult))
            vtiny(TS(COLS[:, c2:c2 + 1], COLS[:, c2:c2 + 1], -0.5, 1.5,
                     ALU.mult, ALU.add))
            vtiny(TT(COLS[:, cr:cr + 1], COLS[:, cr:cr + 1], COLS[:, c2:c2 + 1],
                     ALU.mult))

        def transpose16(src_bf, dst, dst_tile_cols, col_off, wait_ve, src_off=0):
            for half in range(2):
                e.wait(T, CH["se"], CH["se"].n)
                e.wait(T, CH["ve"], wait_ve)
                off = 1024 * half
                for j in range(8):
                    k = 8 * half + j
                    e.op(T, lambda t, k=k, j=j, off=off: t.transpose(
                        PS_Sb[:, off + 128 * j:off + 128 * (j + 1)],
                        src_bf[:, src_off + 128 * k:src_off + 128 * (k + 1)],
                        IDENT[:]), CH["pe"])
                e.wait(SE, CH["pe"], CH["pe"].n)
                e.op(SE, lambda s, half=half, off=off: s.copy(
                    dst[:, :].rearrange("p (k c) -> p k c", c=dst_tile_cols)[
                        :, 8 * half:8 * half + 8, col_off:col_off + 128],
                    PS_Sb[:, off:off + 1024].rearrange("p (j c) -> p j c", j=8)),
                    CH["se"])
            return CH["pe"].n

        applied = {}
        stage_pe = {}
        for t in range(NTF):
            xt_ = xst[t % 2]
            if t >= 2:
                e.wait(SY, CH["ve"], applied[t - 2])
            e.dma(SY, xt_[:], ap["x0"][128 * t:128 * (t + 1), :], CH["cx"])
            e.wait(V, CH["cx"], CH["cx"].n)
            own = t < 8
            cneg = t if own else 48
            cr = (8 + t) if own else 49
            ln_stats(xt_, cneg, cr)
            stage = KTB if t % 2 == 0 else VHB
            if t >= 2:
                e.wait(V, CH["pe"], stage_pe[t - 2])
            e.op(V, TS(stage[:], xt_[:], COLS[:, cneg:cneg + 1],
                       COLS[:, cr:cr + 1], ALU.add, ALU.mult), CH["ve"])
            ve_stage = CH["ve"].n
            if own:
                e.op(V, TS(MACC[:], xt_[:, 0:1024], COLS[:, cneg:cneg + 1],
                           COLS[:, cr:cr + 1], ALU.add, ALU.mult), CH["ve"])
                e.op(V, TS(sq2[:], xt_[:, 1024:2048], COLS[:, cneg:cneg + 1],
                           COLS[:, cr:cr + 1], ALU.add, ALU.mult), CH["ve"])
                e.op(V, RED(COLS[:, 36:37], MACC[:], ALU.max, True), CH["ve"])
                e.op(V, RED(COLS[:, 37:38], sq2[:], ALU.max, True), CH["ve"])
                VS()
                vtiny(TT(COLS[:, 36:37], COLS[:, 36:37], COLS[:, 37:38],
                         ALU.max))
                vtiny(TS(COLS[:, 16 + t:17 + t], COLS[:, 36:37], 1e-5, None,
                         ALU.max))
                vtiny(lambda v, t=t: v.reciprocal(COLS[:, 38:39],
                                                  COLS[:, 16 + t:17 + t]))
                vtiny(TS(COLS[:, 24 + t:25 + t], COLS[:, 38:39], 127.0, None,
                         ALU.mult))
            applied[t] = CH["ve"].n
            if paranoid:
                e.bar()
            stage_pe[t] = transpose16(stage, xnT, S, 128 * t, ve_stage)
            if paranoid:
                e.bar()

        dump("xnT", xnT[:, :])
        dump("cols", COLS[:, :])
        if stop_after == "ph1":
            e.bar()
            e.wait(G, CH["st"], CH["st"].n)
            return
        stat_ve = {}
        ntiles = {"w1T": 64, "w2T": 64}
        idx = 0
        for wi, wname in ((0, "w1T"), (1, "w2T")):
            n_r = ap[wname].shape[0] // 128
            n_c = ap[wname].shape[1] // 2048
            for j in range(n_r * n_c):
                buf = WSTv[idx % 2]
                if idx >= 2:
                    e.wait(SY, CH["ve"], stat_ve[idx - 2])
                else:
                    e.wait(SY, CH["ve"], applied[14 + idx])
                rr, cc = j % n_r, j // n_r
                e.dma(SY, buf[:], ap[wname][128 * rr:128 * (rr + 1),
                                            2048 * cc:2048 * (cc + 1)], CH["cw"])
                e.wait(V, CH["cw"], CH["cw"].n)
                e.op(V, RED(COLS[:, 36:37], buf[:], ALU.add, True), CH["ve"])
                VS()
                vtiny(TT(COLS[:, 40 + wi:41 + wi], COLS[:, 40 + wi:41 + wi],
                         COLS[:, 36:37], ALU.add))
                stat_ve[idx] = CH["ve"].n
                idx += 1
        e.wait(T, CH["ve"], CH["ve"].n)
        e.op(T, MM(PS_ROW[:, 0:2], ONECOL[:, 0:1], COLS[:, 40:42], True, True),
             CH["pe"])
        e.wait(V, CH["pe"], CH["pe"].n)
        vtiny(TS(ZR[:, 0:2], PS_ROW[:, 0:2], 1.0 / (D * FF), 1e-5,
                 ALU.mult, ALU.max))
        vtiny(lambda v: v.reciprocal(ZR[:, 2:4], ZR[:, 0:2]))
        e.wait(T, CH["ve"], CH["ve"].n)
        e.op(T, MM(PS_AUX[:, 0:4], ONE1F[:], ZR[:, 0:4], True, True), CH["pe"])
        e.wait(V, CH["pe"], CH["pe"].n)
        vtiny(lambda v: v.tensor_copy(COLS[:, 46:48], PS_AUX[:, 0:2]))
        vtiny(lambda v: v.tensor_copy(COLS[:, 44:46], PS_AUX[:, 2:4]))
        e.bar()
        if stop_after == "ph1c":
            dump("cols", COLS[:, :])
            e.wait(G, CH["st"], CH["st"].n)
            return

        e.wait(SY, CH["ve"], CH["ve"].n)
        e.dma(SY, cosT_s[:], ap["cosT"], CH["cx"])
        e.dma(SY, sinTn_s[:], ap["sinTn"], CH["cx"])
        tables_cx = CH["cx"].n

        def load_head_w(h, buf):
            for wn, off in (("wqT", 0), ("wkT", 2048)):
                e.dma(G, buf[:, off:off + 2048].rearrange("p (k c) -> p k c", k=16),
                      ap[wn].rearrange("(k p) c -> p k c", p=128)[
                          :, :, 128 * h:128 * (h + 1)], CH["cwg"])

        load_head_w(0, WQB[0])
        cw_head = {0: CH["cwg"].n}
        rope_done = {}
        attn_evac = {}
        for h in range(NH):
            grp, buf = h // 4, WQB[h % 2]
            if h % 4 == 0:
                if h > 0:
                    e.wait(G, CH["pe"], v4_build_pe)
                e.dma(G, AB3[:, 8192:16384].rearrange("p (k c) -> p k c", k=16),
                      ap["wvT"].rearrange("(k p) c -> p k c", p=128)[
                          :, :, 512 * grp:512 * (grp + 1)], CH["cwg"])
                grp_cw = CH["cwg"].n
            if h + 1 < NH:
                load_head_w(h + 1, WQB[(h + 1) % 2])
                cw_head[h + 1] = CH["cwg"].n
            if h % 4 == 0:
                e.wait(T, CH["cwg"], grp_cw)
                if h > 0:
                    e.wait(T, CH["ve"], attn_evac[h - 1])
                v4_se = {}
                for g_t in range(16):
                    pss = (PS_A, PS_B)[g_t % 2]
                    if g_t >= 2:
                        e.wait(T, CH["se"], v4_se[g_t - 2])
                    for k in range(16):
                        e.op(T, MM(pss[:],
                                   xnT[:, k * S + 128 * g_t:k * S + 128 * (g_t + 1)],
                                   AB3[:, 8192 + 512 * k:8192 + 512 * (k + 1)],
                                   k == 0, k == 15), CH["pe"])
                    e.wait(SE, CH["pe"], CH["pe"].n)
                    e.op(SE, lambda s, g_t=g_t, pss=pss: s.copy(
                        AB3[:, 512 * g_t:512 * (g_t + 1)], pss[:]), CH["se"])
                    v4_se[g_t] = CH["se"].n
                v4_build_pe = CH["pe"].n
            e.wait(T, CH["cwg"], cw_head[h])
            if h > 0:
                e.wait(T, CH["ve"], rope_done[h - 1])
                e.wait(T, CH["ve"], attn_evac[h - 1])
            for c in range(2):
                for k in range(16):
                    e.op(T, MM(PS_ACC[:, 512 * c:512 * (c + 1)],
                               buf[:, 128 * k:128 * (k + 1)],
                               xnT[:, k * S + 512 * c:k * S + 512 * (c + 1)],
                               k == 0, k == 15), CH["pe"])
            e.wait(V, CH["pe"], CH["pe"].n)
            e.wait(V, CH["cx"], tables_cx)
            e.op(V, TT(MACC[:], PS_ACC[:], cosT_s[:, 0:1024], ALU.mult), CH["ve"])
            e.op(V, TT(ZACC[0:64, :], PS_ACC[64:128, :], sinTn_s[0:64, 0:1024],
                       ALU.mult), CH["ve"])
            e.op(V, TT(ZACC[64:128, :], PS_ACC[0:64, :], sinTn_s[64:128, 0:1024],
                       ALU.mult), CH["ve"])
            e.op(V, TT(QTB[:], MACC[:], ZACC[:], ALU.add), CH["ve"])
            rope_q = CH["ve"].n
            for r in range(2):
                e.wait(T, CH["ve"], CH["ve"].n)
                for c in range(2):
                    for k in range(16):
                        e.op(T, MM(PS_ACC[:, 512 * c:512 * (c + 1)],
                                   buf[:, 2048 + 128 * k:2048 + 128 * (k + 1)],
                                   xnT[:, k * S + 1024 * r + 512 * c:
                                       k * S + 1024 * r + 512 * (c + 1)],
                                   k == 0, k == 15), CH["pe"])
                e.wait(V, CH["pe"], CH["pe"].n)
                sl = slice(1024 * r, 1024 * (r + 1))
                e.op(V, TT(MACC[:], PS_ACC[:], cosT_s[:, sl], ALU.mult), CH["ve"])
                e.op(V, TT(ZACC[0:64, :], PS_ACC[64:128, :], sinTn_s[0:64, sl],
                           ALU.mult), CH["ve"])
                e.op(V, TT(ZACC[64:128, :], PS_ACC[0:64, :], sinTn_s[64:128, sl],
                           ALU.mult), CH["ve"])
                e.op(V, TT(KTB[:, sl], MACC[:], ZACC[:], ALU.add), CH["ve"])
            rope_done[h] = CH["ve"].n
            e.wait(G, CH["ve"], rope_done[h])
            e.op(G, lambda g: g.memset(ZACC[:], 0.0), CH["ge"])
            ge_z0 = CH["ge"].n
            e.wait(T, CH["ve"], rope_done[h])
            e.wait(T, CH["ge"], ge_z0)
            zadd_ve = 0
            for kp in range(16):
                j = kp % 8
                qs = 128 * j
                e.wait(T, CH["se"], CH["se"].n)
                e.wait(T, CH["ve"], zadd_ve)
                for c in (0, 512):
                    lo = max(qs, c)
                    if lo >= c + 512:
                        continue
                    e.op(T, MM(PS_S[:, lo:c + 512], KTB[:, 128 * kp:128 * (kp + 1)],
                               QTB[:, lo:c + 512], True, True), CH["pe"])
                e.wait(SE, CH["pe"], CH["pe"].n)
                e.op(SE, ACTF(EXPB[:, 0:1024 - qs], PS_S[:, qs:1024], ACT.Exp,
                              scale=ISQ), CH["se"])
                exp_se = CH["se"].n
                e.wait(V, CH["se"], exp_se)
                m = 0 if kp < 8 else 1
                e.op(V, TT(EXPB[:, 0:128], EXPB[:, 0:128],
                           CMASK[:, 128 * m:128 * (m + 1)], ALU.mult), CH["ve"])
                e.op(V, TT(ZACC[:, qs:1024], ZACC[:, qs:1024],
                           EXPB[:, 0:1024 - qs], ALU.add), CH["ve"])
                zadd_ve = CH["ve"].n
                e.wait(T, CH["ve"], zadd_ve)
                for c in (0, 512):
                    lo = max(qs, c)
                    if lo >= c + 512:
                        continue
                    start = (kp == 0)
                    stop = (kp == 11) if c == 0 else (kp == 15)
                    e.op(T, MM(PS_ACC[:, lo:c + 512],
                               AB3[:, 512 * kp + 128 * (h % 4):
                                   512 * kp + 128 * (h % 4) + 128],
                               EXPB[:, lo - qs:c + 512 - qs], start, stop),
                         CH["pe"])
            pe_av = CH["pe"].n
            for c in range(2):
                e.wait(T, CH["ve"], CH["ve"].n)
                e.op(T, MM(PS_ROW[:, 0:512], ONECOL[:, 0:1],
                           ZACC[:, 512 * c:512 * (c + 1)], True, True), CH["pe"])
                e.wait(V, CH["pe"], CH["pe"].n)
                e.op(V, lambda v, c=c: v.reciprocal(ZR[:, 0:512],
                                                    PS_ROW[:, 0:512]), CH["ve"])
                e.wait(T, CH["ve"], CH["ve"].n)
                e.op(T, MM(PS_AUX[:], ONE1F[:], ZR[:, 0:512],
                           True, True), CH["pe"])
                e.wait(V, CH["pe"], CH["pe"].n)
                e.op(V, lambda v, c=c: v.tensor_copy(MACC[:, 512 * c:512 * (c + 1)],
                                                     PS_AUX[:]), CH["ve"])
                e.op(V, TT(aoT[:, SOWN * h + 512 * c:SOWN * h + 512 * (c + 1)],
                           PS_ACC[:, 512 * c:512 * (c + 1)],
                           MACC[:, 512 * c:512 * (c + 1)], ALU.mult),
                     CH["ve"])
            attn_evac[h] = CH["ve"].n
        CROWb = CROW[:, :].bitcast(BF16)
        wo_stage = [(CROWb, 0), (CROWb, 1), (CROWb, 2), (CROWb, 3),
                    (KTB, 0), (KTB, 1), (KTB, 2), (KTB, 3),
                    (VHB, 0), (VHB, 1), (VHB, 2), (VHB, 3),
                    (W1QB, 0), (W1QB, 1), (EXPB, 0), (EXPB, 1)]

        def wo_rhs(k):
            tns, sl = wo_stage[k]
            return tns[:, 512 * sl:512 * (sl + 1)]

        for dc in range(4):
            e.wait(G, CH["ve"], attn_evac[NH - 1])
            e.wait(T, CH["ve"], attn_evac[NH - 1])
            if dc > 0:
                e.wait(G, CH["pe"], CH["pe"].n)
            for tns, k0, nk in ((CROWb, 0, 4), (KTB, 4, 4), (VHB, 8, 4),
                                (W1QB, 12, 2), (EXPB, 14, 2)):
                e.dma(G, tns[:, 0:512 * nk].rearrange("p (k c) -> p k c", k=nk),
                      ap["woT"].rearrange("(k p) c -> p k c", p=128)[
                          :, k0:k0 + nk, 512 * dc:512 * (dc + 1)],
                      CH["cwg"])
            e.wait(T, CH["cwg"], CH["cwg"].n)
            for tt_ in range(8):
                pss = (PS_A, PS_B)[tt_ % 2]
                e.wait(T, CH["se"], CH["se"].n)
                for k in range(16):
                    e.op(T, MM(pss[:],
                               aoT[:, SOWN * k + 128 * tt_:SOWN * k + 128 * (tt_ + 1)],
                               wo_rhs(k), k == 0, k == 15), CH["pe"])
                e.wait(SE, CH["pe"], CH["pe"].n)
                e.op(SE, lambda s, tt_=tt_, dc=dc, pss=pss: s.copy(
                    s0[:, D * tt_ + 512 * dc:D * tt_ + 512 * (dc + 1)], pss[:]),
                    CH["se"])
        dump("aoT", aoT[:, :])
        dump("s0", s0[:, :])
        if stop_after == "attn":
            e.wait(G, CH["st"], CH["st"].n)
            return
        dump("qT", QTB[:, :])
        dump("kT", KTB[:, :])
        dump("v4", AB3[:, 0:8192])
        dump("zacc", ZACC[:, :])
        e.bar()

        for t in range(NT):
            xt_ = xst[t % 2]
            if t >= 2:
                e.wait(SY, CH["ve"], xq_ve[t - 2])
            e.dma(SY, xt_[:], ap["x0"][128 * t:128 * (t + 1), :], CH["cx"])
            e.wait(V, CH["cx"], CH["cx"].n)
            e.op(V, TS(MACC[:], xt_[:, 0:1024], COLS[:, t:t + 1],
                       COLS[:, 8 + t:9 + t], ALU.add, ALU.mult), CH["ve"])
            e.op(V, TS(sq2[:], xt_[:, 1024:2048], COLS[:, t:t + 1],
                       COLS[:, 8 + t:9 + t], ALU.add, ALU.mult), CH["ve"])
            e.op(V, TS(XQI[:, 0:1024], MACC[:], COLS[:, 24 + t:25 + t], None,
                       ALU.mult), CH["ve"])
            e.op(V, TS(XQI[:, 1024:2048], sq2[:], COLS[:, 24 + t:25 + t], None,
                       ALU.mult), CH["ve"])
            if t >= 1:
                e.wait(V, CH["pe"], xq_pe[t - 1])
            e.op(V, lambda v: v.tensor_copy(VHB[:], XQI[:]), CH["ve"])
            if t == 0:
                xq_ve = {}
                xq_pe = {}
            xq_ve[t] = CH["ve"].n
            xq_pe[t] = transpose16(VHB, xqT, SOWN, 128 * t, xq_ve[t])
        dump("xqT", xqT[:, :])
        for j in range(64):
            buf = WSTv[j % 2]
            if j >= 2:
                e.wait(SY, CH["st"], w2q_st[j - 2])
                e.wait(SY, CH["ve"], w2q_ve2[j - 2])
            else:
                e.wait(SY, CH["ve"], xq_ve[6 + j])
            e.dma(SY, buf[:], ap["w2T"][128 * j:128 * (j + 1), :], CH["cw"])
            e.wait(V, CH["cw"], CH["cw"].n)
            if j >= 2:
                e.wait(V, CH["st"], w2q_st[j - 2])
            if j >= 1:
                e.wait(V, CH["st"], w2q_st[j - 1])
            e.op(V, TS(WQI[:], buf[:], COLS[:, 45:46], None, ALU.mult), CH["ve"])
            e.op(V, TS(W1QB[:], WQI[:], 1.0, -1.0, ALU.min, ALU.max), CH["ve"])
            if j == 0:
                w2q_st = {}
                w2q_ve2 = {}
            w2q_ve2[j] = CH["ve"].n
            e.wait(G, CH["ve"], CH["ve"].n)
            e.dma(G, ap["w2q_b"][128 * j:128 * (j + 1), :], W1QB[:], CH["st"])
            w2q_st[j] = CH["st"].n
        e.op(G, lambda g: g.memset(MACC[:], 0.0), CH["ge"])
        ge_m0 = CH["ge"].n
        mm1_se = {}
        mm1_st = {}
        mm1_wq = {}
        mm1_pe = {}
        for fc in range(64):
            buf = WSTv[fc % 2]
            if fc >= 2:
                e.wait(SY, CH["ve"], mm1_wq[fc - 2])
            else:
                e.wait(SY, CH["ve"], w2q_ve2[62 + fc])
            e.dma(SY, buf[:].rearrange("p (k c) -> p k c", k=16),
                  ap["w1T"].rearrange("(k p) c -> p k c", p=128)[
                      :, :, 128 * fc:128 * (fc + 1)], CH["cw"])
            e.wait(V, CH["cw"], CH["cw"].n)
            if fc >= 2:
                e.wait(V, CH["pe"], mm1_pe[fc - 2])
            wq = (W1QB, W1QB2)[fc % 2]
            if fc == 0:
                e.wait(V, CH["st"], w2q_st[63])
            e.op(V, TS(WQI[:], buf[:], COLS[:, 44:45], None, ALU.mult), CH["ve"])
            e.op(V, TS(wq[:], WQI[:], 1.0, -1.0, ALU.min, ALU.max), CH["ve"])
            mm1_wq[fc] = CH["ve"].n
            e.wait(T, CH["ve"], mm1_wq[fc])
            for c in range(2):
                pss = (PS_A, PS_B)[c]
                if fc >= 1:
                    e.wait(T, CH["se"], mm1_se[(fc - 1, c)])
                for k in range(16):
                    e.op(T, MM(pss[:], wq[:, 128 * k:128 * (k + 1)],
                               xqT[:, SOWN * k + 512 * c:SOWN * k + 512 * (c + 1)],
                               k == 0, k == 15), CH["pe"])
                mm1_pe[fc] = CH["pe"].n
                e.wait(SE, CH["pe"], CH["pe"].n)
                e.wait(SE, CH["st"], mm1_st.get((fc - 2, c), 0))
                ub = UB[c]
                e.op(SE, ACTF(ub[:, 0:512], pss[:], ACT.Relu), CH["se"])
                mm1_se[(fc, c)] = CH["se"].n
                e.wait(V, CH["se"], mm1_se[(fc, c)])
                e.wait(V, CH["ge"], ge_m0)
                e.op(V, TT(MACC[:, 512 * c:512 * (c + 1)],
                           MACC[:, 512 * c:512 * (c + 1)], ub[:, 0:512], ALU.max),
                     CH["ve"])
                e.wait(G, CH["ve"], CH["ve"].n)
                e.dma(G, ap["u_b"][128 * fc:128 * (fc + 1), 512 * c:512 * (c + 1)],
                      ub[:, 0:512], CH["st"])
                mm1_st[(fc, c)] = CH["st"].n
        for a in range(8):
            e.wait(T, CH["ve"], CH["ve"].n)
            e.op(T, lambda t, a=a: t.transpose(PS_AUX[:, 0:128],
                                               MACC[:, 128 * a:128 * (a + 1)],
                                               IDENTF[:]), CH["pe"])
            e.wait(V, CH["pe"], CH["pe"].n)
            e.op(V, RED(COLS[:, 32:33], PS_AUX[:, 0:128], ALU.max), CH["ve"])
            VS()
            vtiny(TS(COLS[:, 33:34], COLS[:, 16 + a:17 + a], 1.0 / 127.0, None,
                     ALU.mult))
            vtiny(TT(COLS[:, 33:34], COLS[:, 33:34], COLS[:, 46:47], ALU.mult))
            vtiny(TT(COLS[:, 34:35], COLS[:, 32:33], COLS[:, 33:34], ALU.mult))
            vtiny(TT(COLS[:, 34:35], COLS[:, 34:35], COLS[:, 34:35], ALU.mult))
            vtiny(TS(COLS[:, 35:36], COLS[:, 34:35], 1e-5, None, ALU.max))
            vtiny(lambda v: v.reciprocal(COLS[:, 36:37], COLS[:, 35:36]))
            vtiny(TS(COLS[:, 37:38], COLS[:, 36:37], 127.0, None, ALU.mult))
            vtiny(TS(COLS[:, 52 + a:53 + a], COLS[:, 35:36], 1.0 / 127.0, None,
                     ALU.mult))
            vtiny(TT(COLS[:, 52 + a:53 + a], COLS[:, 52 + a:53 + a],
                     COLS[:, 47:48], ALU.mult))
            e.wait(SE, CH["ve"], CH["ve"].n)
            e.op(SE, ACTF(COLS[:, 38:39], COLS[:, 37:38], ACT.Sqrt), CH["se"])
            e.wait(V, CH["se"], CH["se"].n)
            vtiny(TT(COLS[:, 39:40], COLS[:, 33:34], COLS[:, 38:39], ALU.mult))
            e.wait(T, CH["ve"], CH["ve"].n)
            e.op(T, lambda t: t.transpose(PS_AUX[0:1, 128:256], COLS[:, 39:40],
                                          IDENTF[:]), CH["pe"])
            e.wait(V, CH["pe"], CH["pe"].n)
            e.op(V, lambda v: v.tensor_copy(CR8[0:1, :], PS_AUX[0:1, 128:256]),
                 CH["ve"])
            e.wait(T, CH["ve"], CH["ve"].n)
            e.op(T, MM(PS_AUX[:, 256:384], ONE1F[:], CR8[0:1, :], True, True),
                 CH["pe"])
            e.wait(V, CH["pe"], CH["pe"].n)
            e.op(V, lambda v, a=a: v.tensor_copy(CROW[:, 128 * a:128 * (a + 1)],
                                                 PS_AUX[:, 256:384]), CH["ve"])
        ve_crow = CH["ve"].n
        dump("crow", CROW[:, :])
        if "ub" in debug:
            e.wait(G, CH["st"], CH["st"].n)
            for f in range(64):
                e.dma(G, ap["dbg_ub"][128 * f:128 * (f + 1), :],
                      ap["u_b"][128 * f:128 * (f + 1), :], CH["st"])
        mm2_groups = [PS_A[:, :], PS_B[:, :], PS_ACC[:, 0:512], PS_ACC[:, 512:1024]]
        for ht in range(2):
            tc = slice(512 * ht, 512 * (ht + 1))
            nq_ve = {}
            for f in range(64):
                ub = UB[f % 2]
                if f >= 2:
                    e.wait(SY, CH["ve"], nq_ve[f - 2])
                elif ht == 0:
                    e.wait(SY, CH["st"], mm1_st[(63, f)])
                e.dma(SY, ub[:, 0:512], ap["u_b"][128 * f:128 * (f + 1), tc],
                      CH["cx"])
                e.wait(V, CH["cx"], CH["cx"].n)
                if ht == 0 and f == 0:
                    e.wait(V, CH["ve"], ve_crow)
                e.op(V, TT(ub[:, 512:1024], ub[:, 0:512], CROW[:, tc], ALU.mult),
                     CH["ve"])
                e.op(V, TT(XQI[:, 0:512], ub[:, 512:1024], ub[:, 512:1024],
                           ALU.mult), CH["ve"])
                e.op(V, lambda v, f=f: v.tensor_copy(
                    AB1[:, 512 * f:512 * (f + 1)], XQI[:, 0:512]), CH["ve"])
                nq_ve[f] = CH["ve"].n
            ve_nq = CH["ve"].n
            if ht == 0:
                dump("n2", AB1[:, :])
            for dc in range(4):
                if ht > 0 or dc > 0:
                    e.wait(SY, CH["ve"], gate_ve_last)
                for fb in range(16):
                    buf = (W1QB, KTB)[fb % 2]
                    if fb >= 2:
                        e.wait(SY, CH["pe"], mm2_pe[fb - 2])
                    elif ht == 0 and dc == 0:
                        e.wait(SY, CH["pe"], mm1_pe[63])
                    e.dma(SY, buf[:].rearrange("p (a c) -> p a c", a=4),
                          ap["w2q_b"].rearrange("(a p) c -> p a c", p=128)[
                              :, 4 * fb:4 * fb + 4, 512 * dc:512 * (dc + 1)],
                          CH["cw"])
                    e.wait(T, CH["cw"], CH["cw"].n)
                    if fb == 0:
                        e.wait(T, CH["ve"], ve_nq)
                        if ht > 0 or dc > 0:
                            e.wait(T, CH["ve"], gate_ve_last)
                    for ff in range(4):
                        f = 4 * fb + ff
                        for ts in range(4):
                            e.op(T, MM(mm2_groups[ts],
                                       AB1[:, 512 * f + 128 * ts:
                                           512 * f + 128 * (ts + 1)],
                                       buf[:, 512 * ff:512 * (ff + 1)],
                                       f == 0, f == 63), CH["pe"])
                    mm2_pe = mm2_pe if (fb > 0 or dc > 0 or ht > 0) else {}
                    mm2_pe[fb] = CH["pe"].n
                pe_mm2 = CH["pe"].n
                for ts in range(4):
                    a = 4 * ht + ts
                    e.wait(SE, CH["pe"], pe_mm2)
                    e.wait(SE, CH["ve"], ve_crow)
                    if ht > 0 or dc > 0 or ts > 0:
                        e.wait(SE, CH["ve"], gate_ve_last)
                    e.op(SE, lambda s, ts=ts, a=a: s.activation(
                        out=MACC[:, 0:512], in_=mm2_groups[ts], func=ACT.Silu,
                        scale=COLS[:, 52 + a:53 + a]), CH["se"])
                    se_g0 = CH["se"].n
                    if ht > 0 or dc > 0 or ts > 0:
                        e.wait(SY, CH["ve"], gate_ve_last)
                    e.dma(SY, ZACC[:, 0:512],
                          ap["x0"][128 * a:128 * (a + 1), 512 * dc:512 * (dc + 1)],
                          CH["cx"])
                    cx_x0 = CH["cx"].n
                    e.wait(V, CH["se"], se_g0)
                    e.wait(V, CH["cx"], cx_x0)
                    if ht > 0 or dc > 0 or ts > 0:
                        e.wait(V, CH["st"], st_gate_last)
                    e.op(V, TT(MACC[:, 512:1024],
                               s0[:, D * a + 512 * dc:D * a + 512 * (dc + 1)],
                               MACC[:, 0:512], ALU.mult), CH["ve"])
                    e.op(V, TT(ZACC[:, 512:1024], ZACC[:, 0:512],
                               MACC[:, 512:1024], ALU.add), CH["ve"])
                    ve_x1 = CH["ve"].n
                    e.wait(G, CH["ve"], ve_x1)
                    e.dma(G, ap["x1_b"][128 * a:128 * (a + 1),
                                        512 * dc:512 * (dc + 1)],
                          ZACC[:, 512:1024], CH["st"])
                    if stop_after == "x1":
                        e.dma(G, ap["out"][128 * a:128 * (a + 1),
                                           512 * dc:512 * (dc + 1)],
                              ZACC[:, 512:1024], CH["st"])
                    st_gate_last = CH["st"].n
                    gate_ve_last = CH["ve"].n
        e.bar()
        if stop_after == "x1":
            e.wait(G, CH["st"], CH["st"].n)
            return

        ln1_app = {}
        ln1_pe = {}
        for t in range(NT):
            xt_ = xst[t % 2]
            if t >= 2:
                e.wait(SY, CH["ve"], ln1_app[t - 2])
            e.dma(SY, xt_[:], ap["x1_b"][128 * t:128 * (t + 1), :], CH["cx"])
            e.wait(V, CH["cx"], CH["cx"].n)
            ln_stats(xt_, t, 8 + t)
            stage = KTB if t % 2 == 0 else VHB
            if t >= 2:
                e.wait(V, CH["pe"], ln1_pe[t - 2])
            e.op(V, TS(stage[:], xt_[:], COLS[:, t:t + 1],
                       COLS[:, 8 + t:9 + t], ALU.add, ALU.mult), CH["ve"])
            ln1_app[t] = CH["ve"].n
            ln1_pe[t] = transpose16(stage, AB2, 1024, 128 * t, ln1_app[t])
        dump("xn1T", AB2[:, :])
        if stop_after == "xn1T":
            e.bar()
            e.wait(G, CH["st"], CH["st"].n)
            return

        e.bar()
        kv_evac_ve = 0
        for dc in ([] if skip5 else range(4)):
            if dc > 0:
                e.wait(G, CH["pe"], CH["pe"].n)
            e.dma(G, AB3[:, 0:8192].rearrange("p (k c) -> p k c", k=16),
                  ap["l1kwT"].rearrange("(k p) c -> p k c", p=128)[
                      :, :, 512 * dc:512 * (dc + 1)], CH["cwg"])
            e.dma(G, AB3[:, 8192:16384].rearrange("p (k c) -> p k c", k=16),
                  ap["l1vwT"].rearrange("(k p) c -> p k c", p=128)[
                      :, :, 512 * dc:512 * (dc + 1)], CH["cwg"])
            cw_w = CH["cwg"].n
            for bn, dst in (("l1kb", UB[0]), ("l1vb", UB[1])):
                e.wait(SY, CH["pe"], CH["pe"].n)
                e.dma(SY, BROWF[:], ap[bn][0:1, 512 * dc:512 * (dc + 1)],
                      CH["cx"])
                e.wait(T, CH["cx"], CH["cx"].n)
                e.wait(T, CH["ve"], CH["ve"].n)
                e.op(T, MM(PS_AUX[:, 0:512], ONE1F[:], BROWF[:], True, True),
                     CH["pe"])
                e.wait(V, CH["pe"], CH["pe"].n)
                e.op(V, lambda v, dst=dst: v.tensor_copy(dst[:, 0:512],
                                                         PS_AUX[:, 0:512]),
                     CH["ve"])
            bias_ve = CH["ve"].n
            for tt in range(8):
                e.wait(T, CH["cwg"], cw_w)
                e.wait(T, CH["ve"], kv_evac_ve)
                e.wait(T, CH["ve"], bias_ve)
                for k in range(16):
                    e.op(T, MM(PS_A[:], AB2[:, 1024 * k + 128 * tt:
                                            1024 * k + 128 * (tt + 1)],
                               AB3[:, 512 * k:512 * (k + 1)],
                               k == 0, k == 15), CH["pe"])
                for k in range(16):
                    e.op(T, MM(PS_B[:], AB2[:, 1024 * k + 128 * tt:
                                            1024 * k + 128 * (tt + 1)],
                               AB3[:, 8192 + 512 * k:8192 + 512 * (k + 1)],
                               k == 0, k == 15), CH["pe"])
                pe_kv = CH["pe"].n
                e.wait(V, CH["pe"], pe_kv)
                e.op(V, TT(AB1[:, 16384 + 2048 * tt + 512 * dc:
                               16384 + 2048 * tt + 512 * (dc + 1)],
                           PS_B[:], UB[1][:, 0:512], ALU.add), CH["ve"])
                e.op(V, TT(MACC[:, 0:512], PS_A[:], UB[0][:, 0:512], ALU.add),
                     CH["ve"])
                ve_z = CH["ve"].n
                e.wait(SE, CH["ve"], ve_z)
                e.op(SE, ACTF(ZACC[:, 0:512], MACC[:, 0:512], ACT.Relu),
                     CH["se"])
                e.op(V, TS(MACC[:, 512:1024], MACC[:, 0:512], 0.0, None,
                           ALU.min), CH["ve"])
                ve_m = CH["ve"].n
                e.wait(SE, CH["ve"], ve_m)
                e.op(SE, ACTF(CROW[:, 0:512], MACC[:, 512:1024], ACT.Exp),
                     CH["se"])
                e.wait(V, CH["se"], CH["se"].n)
                e.op(V, TT(AB1[:, 2048 * tt + 512 * dc:
                               2048 * tt + 512 * (dc + 1)],
                           ZACC[:, 0:512], CROW[:, 0:512], ALU.add), CH["ve"])
                kv_evac_ve = CH["ve"].n
        dump("kpv", AB1[:, :])
        kvh_ve = {}
        kvh_st = {}
        for h in ([] if skip5 else range(NH)):
            e.wait(T, CH["ve"], kv_evac_ve)
            if h > 0:
                e.wait(T, CH["ve"], kvh_ve[h - 1])
            for tt in range(8):
                e.op(T, MM(PS_ACC[:, 0:128],
                           AB1[:, 2048 * tt + 128 * h:2048 * tt + 128 * (h + 1)],
                           AB1[:, 16384 + 2048 * tt + 128 * h:
                               16384 + 2048 * tt + 128 * (h + 1)],
                           tt == 0, tt == 7), CH["pe"])
            e.wait(V, CH["pe"], CH["pe"].n)
            sl_ = 128 * (h % 2)
            if h >= 2:
                e.wait(V, CH["st"], kvh_st[h - 2])
            e.op(V, lambda v, sl_=sl_: v.tensor_copy(
                MACC[:, sl_:sl_ + 128], PS_ACC[:, 0:128]), CH["ve"])
            kvh_ve[h] = CH["ve"].n
            e.wait(G, CH["ve"], kvh_ve[h])
            e.dma(G, ap["kv_in"][128 * h:128 * (h + 1), :],
                  MACC[:, sl_:sl_ + 128], CH["st"])
            kvh_st[h] = CH["st"].n
        for c4 in ([] if skip5 else range(4)):
            e.wait(T, CH["ve"], CH["ve"].n)
            for tt in range(8):
                e.op(T, MM(PS_ROW[:, 0:512], ONECOLB[:, 0:1],
                           AB1[:, 2048 * tt + 512 * c4:
                               2048 * tt + 512 * (c4 + 1)],
                           tt == 0, tt == 7), CH["pe"])
            e.wait(V, CH["pe"], CH["pe"].n)
            e.wait(V, CH["st"], CH["st"].n)
            vtiny(lambda v: v.tensor_copy(ZR[:, 0:512], PS_ROW[:, 0:512]))
            e.wait(G, CH["ve"], CH["ve"].n)
            for j in range(4):
                e.dma(G, ap["kv_in"][D + 4 * c4 + j:D + 4 * c4 + j + 1, :],
                      ZR[0:1, 128 * j:128 * (j + 1)], CH["st"])
        if "kvin" in debug:
            e.wait(G, CH["st"], CH["st"].n)
            e.dma(G, ap["dbg_kvin"][0:D, :], ap["kv_in"][0:D, :], CH["st"])
            e.dma(G, ap["dbg_kvin"][D:D + NH, :], ap["kv_in"][D:D + NH, :],
                  CH["st"])
        if stop_after == "kv":
            e.bar()
            e.wait(G, CH["st"], CH["st"].n)
            return
        e.wait(G, CH["st"], CH["st"].n)
        if no_coll:
            e.dma(G, ap["kv_out"][:, :], ap["kv_in"][:, :], CH["st"])
        else:
            e.op(G, lambda g: g.collective_compute(
                "AllReduce", ALU.add,
                replica_groups=[[0, 1], [2, 3], [4, 5], [6, 7]],
                ins=[apt["kv_in"].ap().opt()],
                outs=[apt["kv_out"].ap().opt()]), CH["cc"])
        if stop_after == "coll1":
            e.wait(G, CH["cc"], 0 if no_coll else 1)
            if "kvin" in debug:
                e.dma(G, ap["dbg_kvin"][0:D, :], ap["kv_out"][0:D, :],
                      CH["st"])
                e.dma(G, ap["dbg_kvin"][D:D + NH, :],
                      ap["kv_out"][D:D + NH, :], CH["st"])
            e.bar()
            e.wait(G, CH["st"], CH["st"].n)
            return

        def mx_quant_tile(src_bf, width, dst_bf, wait_pe, wait_st=0):
            nb = width // 32
            CR_I = CROW[:, :].bitcast(mybir.dt.int32)
            e.op(V, RED(CROW[:, 0:nb],
                        src_bf.rearrange("p (b k) -> p b k", k=32),
                        ALU.max, True), CH["ve"])
            VS()
            vtiny(TS(CR_I[:, nb:2 * nb], CROW[:, 0:nb].bitcast(mybir.dt.int32),
                     0x7F800000, None, ALU.bitwise_and))
            vtiny(TS(CR_I[:, 2 * nb:3 * nb], CR_I[:, nb:2 * nb],
                     float(7 << 23), -1.0, ALU.max, ALU.mult))
            vtiny(TS(CR_I[:, 2 * nb:3 * nb], CR_I[:, 2 * nb:3 * nb],
                     float(260 << 23), None, ALU.add))
            vtiny(TS(CR_I[:, 3 * nb:4 * nb], CR_I[:, 2 * nb:3 * nb],
                     -1.0, float(254 << 23), ALU.mult, ALU.add))
            e.op(V, lambda v: v.tensor_tensor(
                out=XQI[:, 0:width].rearrange("p (b k) -> p b k", k=32),
                in0=src_bf.rearrange("p (b k) -> p b k", k=32),
                in1=CROW[:, 2 * nb:3 * nb, None].broadcast_to([128, nb, 32]),
                op=ALU.mult), CH["ve"])
            e.wait(SE, CH["ve"], CH["ve"].n)
            e.wait(SE, CH["pe"], wait_pe)
            e.wait(SE, CH["st"], wait_st)
            e.op(SE, lambda s: s.copy(dst_bf, XQI[:, 0:width]), CH["se"])
            e.wait(V, CH["se"], CH["se"].n)
            e.op(V, lambda v: v.tensor_tensor(
                out=dst_bf.rearrange("p (b k) -> p b k", k=32),
                in0=dst_bf.rearrange("p (b k) -> p b k", k=32),
                in1=CROW[:, 3 * nb:4 * nb, None].broadcast_to([128, nb, 32]),
                op=ALU.mult), CH["ve"])
            return CH["ve"].n

        e.bar()
        wq_ve = {}
        wq_st = {}
        for j in ([] if skip55 else range(16)):
            buf = WQB[j % 2]
            bufF = buf[:, :].bitcast(F32)
            if j >= 2:
                e.wait(SY, CH["ve"], wq_ve[j - 2])
            if j < 8:
                src = ap["w1s"][128 * j:128 * (j + 1), :]
            else:
                r_, cc_ = (j - 8) // 4, (j - 8) % 4
                src = ap["w2s"][128 * r_:128 * (r_ + 1),
                                2048 * cc_:2048 * (cc_ + 1)]
            e.dma(SY, bufF, src, CH["cw"])
            e.wait(V, CH["cw"], CH["cw"].n)
            wq_ve[j] = mx_quant_tile(bufF, 2048, W1QB[:, :], 0,
                                     wait_st=(wq_st[j - 1] if j >= 1 else 0))
            e.wait(G, CH["ve"], wq_ve[j])
            if j < 8:
                dst = ap["ag1_in"][128 * j:128 * (j + 1), :]
            else:
                r_, cc_ = (j - 8) // 4, (j - 8) % 4
                dst = ap["ag2_in"][128 * r_:128 * (r_ + 1),
                                   2048 * cc_:2048 * (cc_ + 1)]
            e.dma(G, dst, W1QB[:, :], CH["st"])
            wq_st[j] = CH["st"].n
        e.wait(G, CH["st"], CH["st"].n)
        if no_coll:
            e.dma(G, ap["ag1_out"][0:1024, :], ap["ag1_in"][:, :], CH["st"])
            e.dma(G, ap["ag2_out"][0:256, :], ap["ag2_in"][:, :], CH["st"])
        else:
            e.op(G, lambda g: g.collective_compute(
                "AllGather", ALU.bypass,
                replica_groups=[[0, 1, 2, 3, 4, 5, 6, 7]],
                ins=[apt["ag1_in"].ap().opt()],
                outs=[apt["ag1_out"].ap().opt()]), CH["cc"])
            e.op(G, lambda g: g.collective_compute(
                "AllGather", ALU.bypass,
                replica_groups=[[0, 1, 2, 3, 4, 5, 6, 7]],
                ins=[apt["ag2_in"].ap().opt()],
                outs=[apt["ag2_out"].ap().opt()]), CH["cc"])
        if stop_after == "coll2":
            e.wait(G, CH["cc"], 0 if no_coll else 3)
            e.bar()
            e.wait(G, CH["st"], CH["st"].n)
            return

        e.bar()
        e.dma(G, WQB[0][:, 0:2048].rearrange("p (k c) -> p k c", k=16),
              ap["l1qwT"].rearrange("(k p) c -> p k c", p=128)[:, :, 0:128],
              CH["cwg"])
        qw_cw = {0: CH["cwg"].n}
        qp_ve = {}
        for i in range(qp_n):
            buf = WQB[i % 2]
            if i + 1 < qp_n:
                if i >= 1:
                    e.wait(G, CH["pe"], qp_pe[i - 1])
                e.dma(G, WQB[(i + 1) % 2][:, 0:2048].rearrange(
                    "p (k c) -> p k c", k=16),
                    ap["l1qwT"].rearrange("(k p) c -> p k c", p=128)[
                        :, :, 128 * (i + 1):128 * (i + 2)], CH["cwg"])
                qw_cw[i + 1] = CH["cwg"].n
            e.wait(T, CH["cwg"], qw_cw[i])
            if i > 0:
                e.wait(T, CH["ve"], qp_ve[i - 1])
            for c in range(2):
                for k in range(16):
                    e.op(T, MM(PS_ACC[:, 512 * c:512 * (c + 1)],
                               buf[:, 128 * k:128 * (k + 1)],
                               AB2[:, 1024 * k + 512 * c:
                                   1024 * k + 512 * (c + 1)],
                               k == 0, k == 15), CH["pe"])
            qp_pe = qp_pe if i > 0 else {}
            qp_pe[i] = CH["pe"].n
            e.wait(V, CH["pe"], qp_pe[i])
            if i > 0:
                e.wait(V, CH["se"], CH["se"].n)
            for c in (0, 512):
                e.op(V, TS(CROW[:, c:c + 512], PS_ACC[:, c:c + 512],
                           QBC[:, i:i + 1], None, ALU.add), CH["ve"])
            ve_z = CH["ve"].n
            e.wait(SE, CH["ve"], ve_z)
            e.op(SE, ACTF(ZACC[:, 0:1024], CROW[:, 0:1024], ACT.Relu),
                 CH["se"])
            e.op(V, TS(MACC[:, 0:1024], CROW[:, 0:1024], 0.0, None, ALU.min),
                 CH["ve"])
            ve_m = CH["ve"].n
            e.wait(SE, CH["ve"], ve_m)
            e.op(SE, ACTF(UB[0][:, :], MACC[:, 0:1024], ACT.Exp), CH["se"])
            e.wait(V, CH["se"], CH["se"].n)
            e.op(V, TT(AB1[:, 1024 * i:1024 * (i + 1)], ZACC[:, 0:1024],
                       UB[0][:, :], ALU.add), CH["ve"])
            qp_ve[i] = CH["ve"].n
        dump("qpT", AB1[:, 0:16384])
        if stop_after == "qp":
            e.bar()
            e.wait(G, CH["st"], CH["st"].n)
            return
        e.wait(G, CH["cc"], 0 if no_coll else 1)
        e.wait(G, CH["ve"], CH["ve"].n)
        e.dma(G, KTB[:].rearrange("p (k c) -> p k c", k=16),
              ap["kv_out"][0:D, :].rearrange("(k p) c -> p k c", p=128),
              CH["cwg"])
        e.dma(SY, CR8[:, :], ap["kv_out"][D:D + NH, :], CH["cx"])
        e.wait(T, CH["cx"], CH["cx"].n)
        e.wait(T, CH["ve"], CH["ve"].n)
        e.op(T, lambda t: t.transpose(PS_AUX[:, 0:16], CR8[:, :],
                                      IDENTF[0:16, 0:16]), CH["pe"])
        e.wait(V, CH["pe"], CH["pe"].n)
        KPCb = COLS[:, 56:64].bitcast(BF16)
        vtiny(lambda v: v.tensor_copy(KPCb[:, 0:16], PS_AUX[:, 0:16]))
        at_ve = {}
        for h in range(NH):
            e.wait(T, CH["cwg"], CH["cwg"].n)
            e.wait(T, CH["ve"], qp_ve[15])
            e.wait(T, CH["ve"], CH["ve"].n if h == 0 else at_ve[h - 1])
            for c in (0, 512):
                e.op(T, MM(PS_ACC[:, c:c + 512], KTB[:, 128 * h:128 * (h + 1)],
                           AB1[:, 1024 * h + c:1024 * h + c + 512],
                           True, True), CH["pe"])
            for c in (0, 512):
                e.wait(T, CH["ve"], CH["ve"].n)
                e.op(T, MM(PS_ROW[:, 0:512], KPCb[:, h:h + 1],
                           AB1[:, 1024 * h + c:1024 * h + c + 512],
                           True, True), CH["pe"])
                e.wait(V, CH["pe"], CH["pe"].n)
                vtiny(TS(ZR[:, 0:512], PS_ROW[:, 0:512], 1e-6, None, ALU.add))
                vtiny(lambda v: v.reciprocal(BROWF[:, 0:512], ZR[:, 0:512]))
                e.wait(T, CH["ve"], CH["ve"].n)
                e.op(T, MM(PS_AUX[:, 0:512], ONE1F[:], BROWF[:, 0:512],
                           True, True), CH["pe"])
                e.wait(V, CH["pe"], CH["pe"].n)
                e.op(V, lambda v, c=c: v.tensor_copy(MACC[:, c:c + 512],
                                                     PS_AUX[:, 0:512]),
                     CH["ve"])
                e.op(V, TT(AB2[:, 1024 * h + c:1024 * h + c + 512],
                           PS_ACC[:, c:c + 512], MACC[:, c:c + 512],
                           ALU.mult), CH["ve"])
            at_ve[h] = CH["ve"].n
        dump("attnT", AB2[:, :])
        if stop_after == "attn1":
            e.bar()
            e.wait(G, CH["st"], CH["st"].n)
            return
        op_pe = {}
        op_st = {}
        for dc in range(4):
            bo = 16384 + 8192 * (dc % 2)
            if dc >= 2:
                e.wait(G, CH["pe"], op_pe[dc - 2])
            e.dma(G, AB1[:, bo:bo + 8192].rearrange("p (k c) -> p k c", k=16),
                  ap["l1owT"].rearrange("(k p) c -> p k c", p=128)[
                      :, :, 512 * dc:512 * (dc + 1)], CH["cwg"])
            cw_o = CH["cwg"].n
            e.wait(SY, CH["pe"], CH["pe"].n)
            e.wait(SY, CH["ve"], CH["ve"].n)
            e.dma(SY, BROWF[:], ap["l1ob"][0:1, 512 * dc:512 * (dc + 1)],
                  CH["cx"])
            e.wait(T, CH["cx"], CH["cx"].n)
            e.wait(T, CH["ve"], CH["ve"].n)
            e.op(T, MM(PS_AUX[:, 0:512], ONE1F[:], BROWF[:], True, True),
                 CH["pe"])
            e.wait(V, CH["pe"], CH["pe"].n)
            e.op(V, lambda v: v.tensor_copy(UB[0][:, 0:512], PS_AUX[:, 0:512]),
                 CH["ve"])
            ob_ve = CH["ve"].n
            for tt in range(8):
                pss = (PS_A, PS_B)[tt % 2]
                e.wait(T, CH["cwg"], cw_o)
                e.wait(T, CH["ve"], at_ve[NH - 1])
                e.wait(T, CH["ve"], ob_ve)
                if (dc, tt) > (0, 1):
                    e.wait(T, CH["ve"], CH["ve"].n)
                for k in range(16):
                    e.op(T, MM(pss[:], AB2[:, 1024 * k + 128 * tt:
                                           1024 * k + 128 * (tt + 1)],
                               AB1[:, bo + 512 * k:bo + 512 * (k + 1)],
                               k == 0, k == 15), CH["pe"])
                op_pe[dc] = CH["pe"].n
                e.wait(V, CH["pe"], CH["pe"].n)
                sl_ = 512 * (tt % 2)
                if (dc, tt) > (0, 1):
                    e.wait(V, CH["st"], op_st[(dc * 8 + tt) - 2])
                e.op(V, TT(EXPB[:, sl_:sl_ + 512], pss[:], UB[0][:, 0:512],
                           ALU.add), CH["ve"])
                e.wait(G, CH["ve"], CH["ve"].n)
                e.dma(G, ap["s1_b"][128 * tt:128 * (tt + 1),
                                    512 * dc:512 * (dc + 1)],
                      EXPB[:, sl_:sl_ + 512], CH["st"])
                op_st[dc * 8 + tt] = CH["st"].n
        if "s1b" in debug:
            e.wait(G, CH["st"], CH["st"].n)
            e.dma(G, ap["dbg_s1b"][:, :], ap["s1_b"][:, :], CH["st"])
        if stop_after == "s1":
            e.bar()
            e.wait(G, CH["st"], CH["st"].n)
            return

        KTBf = KTB[:, :].bitcast(F32)
        VHBf = VHB[:, :].bitcast(F32)
        W1QBf = W1QB[:, :].bitcast(F32)
        XQIb = XQI[:, :].bitcast(BF16)
        mm2g = [PS_A[:, :], PS_B[:, :], PS_ACC[:, 0:512], PS_ACC[:, 512:1024]]
        for half in range(2):
            e.bar()
            if half == 0:
                e.wait(SY, CH["cc"], 0 if no_coll else 3)
            for ci, dst in ((0, KTBf[:, 0:512]), (1, KTBf[:, 512:1024]),
                            (2, VHBf[:, 0:512]), (3, VHBf[:, 512:1024]),
                            (4, W1QBf[:, 0:512]), (5, W1QBf[:, 512:1024]),
                            (6, UB[0][:, 0:512]), (7, UB[0][:, 512:1024])):
                nm = "l1lnw" if ci < 4 else "l1lnb"
                co = 512 * (ci % 4)
                e.wait(SY, CH["pe"], CH["pe"].n)
                e.dma(SY, BROWF[:], ap[nm][0:1, co:co + 512], CH["cx"])
                e.wait(T, CH["cx"], CH["cx"].n)
                e.wait(T, CH["ve"], CH["ve"].n)
                e.op(T, MM(PS_AUX[:, 0:512], ONE1F[:], BROWF[:], True, True),
                     CH["pe"])
                e.wait(V, CH["pe"], CH["pe"].n)
                e.op(V, lambda v, dst=dst: v.tensor_copy(dst, PS_AUX[:, 0:512]),
                     CH["ve"])
            rep_ve = CH["ve"].n
            xm_pe = {}
            for tt in range(4):
                t = 4 * half + tt
                if tt > 0:
                    e.wait(SY, CH["ve"], CH["ve"].n)
                e.dma(SY, WQB[0][:, :].bitcast(F32),
                      ap["x1_b"][128 * t:128 * (t + 1), :], CH["cx"])
                e.wait(V, CH["cx"], CH["cx"].n)
                W0f = WQB[0][:, :].bitcast(F32)
                e.op(V, TS(MACC[:], W0f[:, 0:1024], COLS[:, t:t + 1],
                           COLS[:, 8 + t:9 + t], ALU.add, ALU.mult), CH["ve"])
                e.op(V, TS(ZACC[:], W0f[:, 1024:2048], COLS[:, t:t + 1],
                           COLS[:, 8 + t:9 + t], ALU.add, ALU.mult), CH["ve"])
                xbf_t = XBFv[:, 2048 * tt:2048 * (tt + 1)]
                e.op(V, lambda v, xbf_t=xbf_t: v.tensor_copy(
                    xbf_t[:, 0:1024], MACC[:, :]), CH["ve"])
                e.op(V, lambda v, xbf_t=xbf_t: v.tensor_copy(
                    xbf_t[:, 1024:2048], ZACC[:, :]), CH["ve"])
                ln_stats(xbf_t, 36, 37)
                e.wait(V, CH["ve"], rep_ve)
                e.op(V, TS(MACC[:], xbf_t[:, 0:1024], COLS[:, 36:37],
                           COLS[:, 37:38], ALU.add, ALU.mult), CH["ve"])
                e.op(V, TS(ZACC[:], xbf_t[:, 1024:2048], COLS[:, 36:37],
                           COLS[:, 37:38], ALU.add, ALU.mult), CH["ve"])
                e.op(V, TT(MACC[:], MACC[:], KTBf[:, :], ALU.mult), CH["ve"])
                e.op(V, TT(ZACC[:], ZACC[:], VHBf[:, :], ALU.mult), CH["ve"])
                e.op(V, TT(WQB[1][:, 0:1024], MACC[:], W1QBf[:, :], ALU.add),
                     CH["ve"])
                e.op(V, TT(WQB[1][:, 1024:2048], ZACC[:], UB[0][:, :], ALU.add),
                     CH["ve"])
                if tt == 0:
                    wait_pe = 0 if half == 0 else prev_xm_pe
                else:
                    wait_pe = xm_pe[tt - 1]
                q_ve = mx_quant_tile(WQB[1][:, 0:2048], 2048,
                                     WQB[1][:, 2048:4096], wait_pe)
                xm_pe[tt] = transpose16(WQB[1], AB2, 512, 128 * tt, q_ve,
                                        src_off=2048)
            prev_xm_pe = xm_pe[3]
            if half == 0:
                dump("xmxT", AB2[:, :])
            e.bar()
            mm1_ev = 0
            mm1_g = {}
            w1b_pe = {}
            for fc in range(16):
                wbo = 8192 * (fc % 2)
                if fc >= 2:
                    e.wait(SY, CH["pe"], w1b_pe[fc - 2])
                for dt in range(16):
                    e.dma(SY, AB3[:, wbo + 512 * dt:wbo + 512 * (dt + 1)],
                          ap["ag1_out"][512 * fc:512 * (fc + 1),
                                        128 * dt:128 * (dt + 1)],
                          CH["cw"], transpose=True)
                cw_m = CH["cw"].n
                e.wait(SY, CH["pe"], CH["pe"].n)
                e.wait(SY, CH["ve"], CH["ve"].n)
                e.dma(SY, BROWF[:], ap["l1b1"][0:1, 512 * fc:512 * (fc + 1)],
                      CH["cx"])
                e.wait(T, CH["cx"], CH["cx"].n)
                e.wait(T, CH["ve"], CH["ve"].n)
                e.op(T, MM(PS_AUX[:, 0:512], ONE1F[:], BROWF[:], True, True),
                     CH["pe"])
                e.wait(V, CH["pe"], CH["pe"].n)
                e.op(V, lambda v: v.tensor_copy(CROW[:, 512:1024],
                                                PS_AUX[:, 0:512]), CH["ve"])
                b1_ve = CH["ve"].n
                for tt in range(4):
                    e.wait(T, CH["cw"], cw_m)
                    e.wait(T, CH["ve"], mm1_g.get(tt, 0))
                    e.wait(T, CH["ve"], b1_ve)
                    for dt in range(16):
                        e.op(T, MM(mm2g[tt],
                                   AB2[:, 512 * dt + 128 * tt:
                                       512 * dt + 128 * (tt + 1)],
                                   AB3[:, wbo + 512 * dt:wbo + 512 * (dt + 1)],
                                   dt == 0, dt == 15), CH["pe"])
                    w1b_pe[fc] = CH["pe"].n
                    e.wait(V, CH["pe"], CH["pe"].n)
                    e.wait(V, CH["se"], CH["se"].n)
                    e.op(V, TT(EXPB[:, 0:512], mm2g[tt], CROW[:, 512:1024],
                               ALU.add), CH["ve"])
                    mm1_g[tt] = CH["ve"].n
                    e.wait(SE, CH["ve"], mm1_g[tt])
                    e.op(SE, ACTF(MACC[:, 0:512], EXPB[:, 0:512],
                                  ACT.Gelu_apprx_tanh), CH["se"])
                    e.op(SE, lambda s: s.copy(EXPB[:, 512:1024],
                                              MACC[:, 0:512]), CH["se"])
                    gel_se = CH["se"].n
                    e.wait(V, CH["se"], gel_se)
                    q2_ve = mx_quant_tile(EXPB[:, 512:1024], 512,
                                          XQIb[:, 512:1024], CH["pe"].n)
                    e.wait(T, CH["ve"], q2_ve)
                    e.wait(T, CH["se"], CH["se"].n)
                    for jj in range(4):
                        e.op(T, lambda t, jj=jj: t.transpose(
                            PS_Sb[:, 128 * jj:128 * (jj + 1)],
                            XQIb[:, 512 + 128 * jj:640 + 128 * jj],
                            IDENT[:]), CH["pe"])
                    e.wait(SE, CH["pe"], CH["pe"].n)
                    e.op(SE, lambda s, fc=fc, tt=tt: s.copy(
                        AB1[:, :].rearrange("p (f c) -> p f c", c=512)[
                            :, 4 * fc:4 * fc + 4, 128 * tt:128 * (tt + 1)],
                        PS_Sb[:, 0:512].rearrange("p (j c) -> p j c", j=4)),
                        CH["se"])
            if half == 0:
                dump("xmx2T", AB1[:, :])
            e.bar()
            gate_ve = 0
            gate_st = 0
            mm2_pe = {}
            for dc in range(4):
                e.wait(SY, CH["pe"], CH["pe"].n)
                e.wait(SY, CH["ve"], CH["ve"].n)
                e.dma(SY, BROWF[:], ap["l1b2"][0:1, 512 * dc:512 * (dc + 1)],
                      CH["cx"])
                e.wait(T, CH["cx"], CH["cx"].n)
                e.wait(T, CH["ve"], CH["ve"].n)
                e.op(T, MM(PS_AUX[:, 0:512], ONE1F[:], BROWF[:], True, True),
                     CH["pe"])
                e.wait(V, CH["pe"], CH["pe"].n)
                e.op(V, lambda v: v.tensor_copy(CROW[:, 512:1024],
                                                PS_AUX[:, 0:512]), CH["ve"])
                b2_ve = CH["ve"].n
                for fb in range(16):
                    bo2 = 8192 + 2048 * (fb % 2)
                    if fb >= 2:
                        e.wait(SY, CH["pe"], mm2_pe[fb - 2])
                    elif dc > 0:
                        e.wait(SY, CH["pe"], mm2_pe[14 + fb])
                    for ff in range(4):
                        ft = 4 * fb + ff
                        e.dma(SY, AB2[:, bo2 + 512 * ff:bo2 + 512 * (ff + 1)],
                              ap["ag2_out"][512 * dc:512 * (dc + 1),
                                            128 * ft:128 * (ft + 1)],
                              CH["cw"], transpose=True)
                    e.wait(T, CH["cw"], CH["cw"].n)
                    if fb == 0:
                        e.wait(T, CH["ve"], b2_ve)
                        if dc > 0:
                            e.wait(T, CH["ve"], gate_ve)
                    for ff in range(4):
                        ft = 4 * fb + ff
                        for ts in range(4):
                            e.op(T, MM(mm2g[ts],
                                       AB1[:, 512 * ft + 128 * ts:
                                           512 * ft + 128 * (ts + 1)],
                                       AB2[:, bo2 + 512 * ff:
                                           bo2 + 512 * (ff + 1)],
                                       ft == 0, ft == 63), CH["pe"])
                    mm2_pe = mm2_pe if (fb > 0 or dc > 0) else {}
                    mm2_pe[fb] = CH["pe"].n
                pe_m2 = CH["pe"].n
                for ts in range(4):
                    t = 4 * half + ts
                    e.wait(V, CH["pe"], pe_m2)
                    e.wait(V, CH["se"], CH["se"].n)
                    e.op(V, TT(EXPB[:, 0:512], mm2g[ts], CROW[:, 512:1024],
                               ALU.add), CH["ve"])
                    e.op(V, TT(EXPB[:, 512:1024],
                               XBFv[:, 2048 * ts + 512 * dc:
                                    2048 * ts + 512 * (dc + 1)],
                               EXPB[:, 0:512], ALU.add), CH["ve"])
                    ve_of = CH["ve"].n
                    e.wait(SE, CH["ve"], ve_of)
                    if gate_ve > 0:
                        e.wait(SE, CH["ve"], gate_ve)
                    e.op(SE, ACTF(MACC[:, 0:512], EXPB[:, 512:1024],
                                  ACT.Silu), CH["se"])
                    se_g1 = CH["se"].n
                    if gate_ve > 0:
                        e.wait(SY, CH["ve"], gate_ve)
                    e.dma(SY, ZACC[:, 0:512],
                          ap["x1_b"][128 * t:128 * (t + 1),
                                     512 * dc:512 * (dc + 1)], CH["cx"])
                    e.dma(SY, XQIb[:, 0:512],
                          ap["s1_b"][128 * t:128 * (t + 1),
                                     512 * dc:512 * (dc + 1)], CH["cx"])
                    cx_g = CH["cx"].n
                    e.wait(V, CH["se"], se_g1)
                    e.wait(V, CH["cx"], cx_g)
                    if gate_st > 0:
                        e.wait(V, CH["st"], gate_st)
                    e.op(V, TT(MACC[:, 512:1024], XQIb[:, 0:512],
                               MACC[:, 0:512], ALU.mult), CH["ve"])
                    e.op(V, TT(ZACC[:, 512:1024], ZACC[:, 0:512],
                               MACC[:, 512:1024], ALU.add), CH["ve"])
                    gate_ve = CH["ve"].n
                    e.wait(G, CH["ve"], gate_ve)
                    e.dma(G, ap["out"][128 * t:128 * (t + 1),
                                       512 * dc:512 * (dc + 1)],
                          ZACC[:, 512:1024], CH["st"])
                    gate_st = CH["st"].n
        e.bar()
        e.wait(G, CH["st"], CH["st"].n)

    with nc.Block() as block:
        for nm in CHN:
            if nm in ("cx", "cw", "cwg", "st"):
                chans[nm] = DmaGrp(nc, nm, nsub=4)
            else:
                chans[nm] = Ch(nc, nm)

        def run(live, eng):
            for c in chans.values():
                if isinstance(c, DmaGrp):
                    c.reset()
                else:
                    c.n = 0
            program(Emit(nc, live, eng, chans))

        @block.sync
        def _(eng):
            run("sync", eng)

        @block.tensor
        def _(eng):
            run("tensor", eng)

        @block.vector
        def _(eng):
            run("vector", eng)

        @block.scalar
        def _(eng):
            run("scalar", eng)

        @block.gpsimd
        def _(eng):
            run("gpsimd", eng)

    for cm in reversed(ctxs):
        cm.__exit__(None, None, None)
    return nc, ap


def rope_tables():
    inv_freq = 1.0 / 10000.0 ** (np.arange(0, HD, 2, dtype=np.float32) / HD)
    t_ = np.arange(S, dtype=np.float32)
    freqs = np.einsum("i,j->ij", t_, inv_freq)
    emb = np.concatenate((freqs, freqs), -1)
    return np.cos(emb), np.sin(emb)


_COS, _SIN = rope_tables()


def perm_for(p):
    blocks = [2 * j + p for j in range(8)] + [2 * j + 1 - p for j in range(8)]
    idx = np.concatenate([np.arange(128) + 128 * g for g in blocks])
    return np.asarray(blocks), idx


def prep_core_inputs(I, c):
    b, p = c >> 1, c & 1
    blocks, idx = perm_for(p)
    x0 = np.ascontiguousarray(np.asarray(I["L0_attn_hidden_states"][b],
                                         dtype=np.float32)[idx])
    cosP = _COS[idx]
    sinP = _SIN[idx]
    cosT = np.ascontiguousarray(cosP.T)
    sinTn = sinP.T.copy()
    sinTn[0:64] = -sinTn[0:64]
    sinTn = np.ascontiguousarray(sinTn)
    qq = np.arange(128)
    m0 = (qq[:, None] <= qq[None, :]).astype(BF)
    m1 = (np.ones((128, 128)) * (1.0 if p == 1 else 0.0)).astype(BF)
    cmask = np.stack([m0, m1])
    T_ = lambda a: np.ascontiguousarray(np.asarray(a, np.float32).T)
    C_ = lambda a: np.ascontiguousarray(np.asarray(a, np.float32))
    d = {
        "x0": x0,
        "wqT": T_(I["L0_attn_Wq"]), "wkT": T_(I["L0_attn_Wk"]),
        "wvT": T_(I["L0_attn_Wv"]), "woT": T_(I["L0_attn_Wo"]),
        "w1T": T_(I["L0_ffn_w1"]), "w2T": T_(I["L0_ffn_w2"]),
        "l1qwT": T_(I["L1_attn_q_w"]), "l1kwT": T_(I["L1_attn_k_w"]),
        "l1vwT": T_(I["L1_attn_v_w"]), "l1owT": T_(I["L1_attn_o_w"]),
        "l1qb": np.ascontiguousarray(
            np.asarray(I["L1_attn_q_b"], np.float32).reshape(16, 128).T),
        "l1kb": C_(I["L1_attn_k_b"]).reshape(1, D),
        "l1vb": C_(I["L1_attn_v_b"]).reshape(1, D),
        "l1ob": C_(I["L1_attn_o_b"]).reshape(1, D),
        "l1lnw": C_(I["L1_ffn_ln_w"]).reshape(1, D),
        "l1lnb": C_(I["L1_ffn_ln_b"]).reshape(1, D),
        "w1s": C_(I["L1_ffn_W1"][1024 * c:1024 * (c + 1)]),
        "l1b1": C_(I["L1_ffn_b1"]).reshape(1, FF),
        "w2s": C_(I["L1_ffn_W2"][256 * c:256 * (c + 1)]),
        "l1b2": C_(I["L1_ffn_b2"]).reshape(1, D),
        "cosT": cosT, "sinTn": sinTn,
        "ident": np.eye(128, dtype=BF),
        "identf": np.eye(128, dtype=np.float32),
        "cmask": cmask,
    }
    return d


def assemble(results, out_dtype=np.float32):
    out = np.zeros((4, S, D), dtype=out_dtype)
    for c in range(8):
        b, p = c >> 1, c & 1
        _, idx = perm_for(p)
        out[b, idx[:SOWN]] = results[c]["out"]
    return out


def kernel(**inputs):
    nc, _ap = build()
    in_maps = [prep_core_inputs(inputs, c) for c in range(8)]
    res = run_bass_kernel_spmd(nc, in_maps, core_ids=list(range(8)))
    return assemble([res.results[c] for c in range(8)])
